# version 11
# baseline (speedup 1.0000x reference)
import numpy as np
from contextlib import ExitStack

import concourse.bass as bass
import concourse.mybir as mybir
from concourse.bass_utils import run_bass_kernel_spmd

F32 = mybir.dt.float32
BF16 = mybir.dt.bfloat16
AF = mybir.ActivationFunctionType
ALU = mybir.AluOpType
AX = mybir.AxisListType

ENGINES = ("sync", "gpsimd", "tensor", "vector", "scalar")

BUILD_TO = "all"
N_A1 = 32
A4_STAGE = 3
DBG = {}


class P:

    def __init__(self, nc, block):
        self.nc = nc
        self.block = block
        self.buf = {e: [] for e in ENGINES}
        self.cnt = {}
        self.sems = {}
        self.hwm = {e: {} for e in ENGINES}

    def sem(self, name, stack):
        h = stack.enter_context(self.nc.semaphore(name))
        self.sems[name] = h
        self.cnt[name] = 0

    SERIAL = False

    def do(self, eng, fn, sig=None, inc=1):
        if P.SERIAL:
            for s in list(self.cnt):
                self.wait(eng, (s, self.cnt[s], "_none_", 2))
        if sig is None:
            self.buf[eng].append(("op", fn, None, 0))
            return None
        self.cnt[sig] += inc
        val = self.cnt[sig]
        self.buf[eng].append(("op", fn, sig, inc))
        return (sig, val, eng, inc)

    def wait(self, eng, *toks):
        for tok in toks:
            if tok is None:
                continue
            sig, val, src, inc = tok
            if inc == 16:
                val = self.cnt[sig]
            if self.hwm[eng].get(sig, 0) >= val:
                continue
            self.hwm[eng][sig] = val
            self.buf[eng].append(("wait", sig, val))

    def barrier_tok(self, sig):
        return (sig, self.cnt[sig], "_none_", 2)

    def flush(self):
        for eng in ENGINES:
            items = self.buf[eng]
            if not items:
                continue
            self.buf[eng] = []
            dec = getattr(self.block, eng)
            sems = self.sems

            def body(e, items=items):
                for it in items:
                    if it[0] == "op":
                        _, fn, sig, inc = it
                        ins = fn(e)
                        if sig is not None:
                            ins.then_inc(sems[sig], inc)
                    else:
                        _, sig, val = it
                        e.wait_ge(sems[sig], val)
            dec(body)


def build_nc():
    nc = bass.Bass("TRN2", target_bir_lowering=False)

    x = nc.dram_tensor("x", [4096, 2048], F32, kind="ExternalInput")
    wq = nc.dram_tensor("wq", [2048, 2048], F32, kind="ExternalInput")
    wk = nc.dram_tensor("wk", [2048, 512], F32, kind="ExternalInput")
    wv = nc.dram_tensor("wv", [2048, 512], F32, kind="ExternalInput")
    wo = nc.dram_tensor("wo", [2048, 2048], F32, kind="ExternalInput")
    upw = nc.dram_tensor("upw", [8192, 2048], F32, kind="ExternalInput")
    dnw = nc.dram_tensor("dnw", [2048, 8192], F32, kind="ExternalInput")
    ups = nc.dram_tensor("ups", [128, 64], F32, kind="ExternalInput")
    dns = nc.dram_tensor("dns", [128, 16], F32, kind="ExternalInput")
    outsc = nc.dram_tensor("outsc", [1, 1], F32, kind="ExternalInput")
    cosk = nc.dram_tensor("cosk", [128, 1024], F32, kind="ExternalInput")
    sink = nc.dram_tensor("sink", [128, 1024], F32, kind="ExternalInput")
    ident = nc.dram_tensor("ident", [128, 128], F32, kind="ExternalInput")
    onesr = nc.dram_tensor("onesr", [1, 128], F32, kind="ExternalInput")
    out = nc.dram_tensor("out", [1024, 2048], F32, kind="ExternalOutput")
    xr2sp = nc.dram_tensor("xr2sp", [1024, 2048], F32, kind="Internal")
    scr_up = nc.dram_tensor("scr_up", [8192, 2048], BF16, kind="Internal")
    r_dram = [nc.dram_tensor(f"r_dram{i}", [1, 1024], F32, kind="Internal") for i in range(2)]
    scr_dn = nc.dram_tensor("scr_dn", [2048, 8192], BF16, kind="Internal")

    dbg_outs = {}
    for name, (shape, dt) in DBG.items():
        dbg_outs[name] = nc.dram_tensor("dbg_" + name, list(shape), dt, kind="ExternalOutput")

    g = ExitStack()
    with g:
        with nc.Block() as block:
            p = P(nc, block)
            DMA_SEMS = ("lc", "lx0", "lx1", "cw", "cq", "co", "cu0", "cu1",
                        "cd0", "cd1", "sp0", "sp1", "so0", "so1", "dbg", "lx2",
                        "f0l0", "f0l1", "f0s0", "f0s1", "tu0", "tu1", "rb0", "rb1")
            for s in DMA_SEMS + ("t", "v", "s"):
                p.sem(s, g)

            def all_dma_barrier(eng):
                for s in DMA_SEMS:
                    if p.cnt[s]:
                        p.wait(eng, (s, p.cnt[s], "_none_", 16))

            def full_barrier():
                for eng in ENGINES:
                    for s_ in ("t", "v", "s"):
                        if p.cnt[s_]:
                            p.wait(eng, (s_, p.cnt[s_], "_none_", 2))
                    all_dma_barrier(eng)

            def dbg_dump(name, ap_fn, *deps):
                if name not in DBG:
                    return
                p.wait("sync", *deps)
                p.do("sync", lambda e, f=ap_fn, d=dbg_outs[name]: e.dma_start(out=d[:], in_=f()),
                     sig="dbg", inc=16)

            def load(dst_ap_fn, src_ap_fn, *, eng="sync", sig="lc"):
                return p.do(eng, lambda e: e.dma_start(out=dst_ap_fn(), in_=src_ap_fn()),
                            sig=sig, inc=16)

            def finish(src_fn):
                for tl in range(8):
                    p.do("sync", lambda e, tl=tl: e.dma_start(
                        out=out[tl * 128:(tl + 1) * 128, :], in_=src_fn()), sig="so0", inc=16)
                all_dma_barrier("sync")
                p.flush()

            id_bf = g.enter_context(nc.sbuf_tensor("id_bf", [128, 128], BF16))
            id_f = g.enter_context(nc.sbuf_tensor("id_f", [128, 128], F32))
            ones_f = g.enter_context(nc.sbuf_tensor("ones_f", [1, 128], F32))
            ups_sb = g.enter_context(nc.sbuf_tensor("ups_sb", [128, 64], F32))
            dns_sb = g.enter_context(nc.sbuf_tensor("dns_sb", [128, 16], F32))
            outs_sb = g.enter_context(nc.sbuf_tensor("outs_sb", [1, 1], F32))
            ebias = g.enter_context(nc.sbuf_tensor("ebias", [128, 1], F32))
            epsb = g.enter_context(nc.sbuf_tensor("epsb", [128, 1], F32))
            stats = g.enter_context(nc.sbuf_tensor("stats", [128, 32], F32))
            sqscr = g.enter_context(nc.sbuf_tensor("sqscr", [128, 2048], F32))
            xbufs = [g.enter_context(nc.sbuf_tensor(f"xbuf{i}", [128, 2048], F32))
                     for i in range(2)]

            t_idb = load(lambda: id_bf[:], lambda: ident[:], eng="gpsimd", sig="cw")
            t_idf = load(lambda: id_f[:], lambda: ident[:])
            t_ones = load(lambda: ones_f[:], lambda: onesr[:])
            t_upsl = load(lambda: ups_sb[:], lambda: ups[:])
            t_dnsl = load(lambda: dns_sb[:], lambda: dns[:])
            t_outsl = load(lambda: outs_sb[:], lambda: outsc[:])
            t_eb = p.do("vector", lambda e: e.memset(ebias[:], -20.0), sig="v")
            t_eps = p.do("vector", lambda e: e.memset(epsb[:], 1e-5), sig="v")

            def layernorm_stats(ap_fn, sl, deps, stats_gate, psum_in=False):
                p.wait("scalar", *(list(deps) + [stats_gate]))
                t_ssq = p.do("scalar", lambda e: e.activation(
                    sqscr[:], ap_fn(), AF.Square, accum_out=stats[:, sl + 1:sl + 2]), sig="s")
                p.wait("vector", *(list(deps) + [stats_gate]))
                if psum_in:
                    p.wait("vector", t_ssq)
                t_sum = p.do("vector", lambda e: e.reduce_sum(
                    stats[:, sl:sl + 1], ap_fn(), axis=AX.X), sig="v")
                p.wait("vector", t_sum)
                t_mean = p.do("vector", lambda e: e.tensor_scalar(
                    stats[:, sl + 2:sl + 3], stats[:, sl:sl + 1], 1.0 / 2048, None, ALU.mult), sig="v")
                p.wait("vector", t_mean)
                t_msq = p.do("vector", lambda e: e.tensor_tensor(
                    stats[:, sl + 3:sl + 4], stats[:, sl + 2:sl + 3], stats[:, sl + 2:sl + 3], ALU.mult), sig="v")
                p.wait("vector", t_msq, t_ssq)
                t_var = p.do("vector", lambda e: e.scalar_tensor_tensor(
                    stats[:, sl + 4:sl + 5], stats[:, sl + 1:sl + 2], 1.0 / 2048,
                    stats[:, sl + 3:sl + 4], ALU.mult, ALU.subtract), sig="v")
                p.wait("scalar", t_var, t_eps)
                t_ln = p.do("scalar", lambda e: e.activation(
                    stats[:, sl + 5:sl + 6], stats[:, sl + 4:sl + 5], AF.Ln, bias=epsb[:]), sig="s")
                p.wait("scalar", t_ln)
                t_rstd = p.do("scalar", lambda e: e.activation(
                    stats[:, sl + 6:sl + 7], stats[:, sl + 5:sl + 6], AF.Exp, scale=-0.5), sig="s")
                return t_var, t_rstd

            sKV = g.enter_context(ExitStack())
            KT = sKV.enter_context(nc.sbuf_tensor("KT", [128, 4 * 4096], BF16))
            VT = sKV.enter_context(nc.sbuf_tensor("VT", [128, 32 * 520], BF16))

            sRope = g.enter_context(ExitStack())
            cos_sb = sRope.enter_context(nc.sbuf_tensor("cos_sb", [128, 1024], F32, side="right"))
            sin_sb = sRope.enter_context(nc.sbuf_tensor("sin_sb", [128, 1024], F32, side="right"))
            kq_bfs = [sRope.enter_context(nc.sbuf_tensor(f"kq_bf{i}", [128, 512], BF16, side="right"))
                      for i in range(2)]
            rp_sh = sRope.enter_context(nc.sbuf_tensor("rp_sh", [128, 512], F32, side="right"))
            rp_t1 = sRope.enter_context(nc.sbuf_tensor("rp_t1", [128, 512], F32, side="right"))
            xnT_own = sRope.enter_context(nc.sbuf_tensor("xnT_own", [128, 16 * 1024], BF16, side="right"))

            t_cos = load(lambda: cos_sb[:], lambda: cosk[:])
            t_sin = load(lambda: sin_sb[:], lambda: sink[:])
            t_vones = p.do("vector", lambda e: e.memset(
                VT[:].rearrange("p (t g c) -> p t g c", g=8, c=65)[:, :, :, 64:65], 1.0), sig="v")

            rp_free = [None]

            def rope(src_psum_fn, ct, deps, dst_gate, kq_bf):
                cosA = bass.AP(cos_sb, ct * 32, [[1024, 128], [0, 8], [1, 32], [0, 2]])
                sinA = bass.AP(sin_sb, ct * 32, [[1024, 128], [0, 8], [1, 32], [0, 2]])
                g8 = lambda ap: ap.rearrange("p (u c) -> p u c", u=8)
                g82 = lambda ap: ap.rearrange("p (u f q) -> p u f q", u=8, q=2)
                p.wait("vector", *(list(deps) + [t_cos, t_sin, rp_free[0]]))
                p.do("vector", lambda e: e.tensor_scalar(
                    g8(rp_sh[:])[:, :, 0:64:2], g8(src_psum_fn())[:, :, 1:64:2],
                    -1.0, None, ALU.mult))
                t_c2 = p.do("vector", lambda e: e.tensor_copy(
                    g8(rp_sh[:])[:, :, 1:64:2], g8(src_psum_fn())[:, :, 0:64:2]), sig="v")
                t_t1 = p.do("vector", lambda e: e.tensor_tensor(
                    g82(rp_t1[:]), g82(src_psum_fn()), cosA, ALU.mult), sig="v")
                p.wait("vector", t_c2)
                t_sm = p.do("vector", lambda e: e.tensor_tensor(
                    g82(rp_sh[:]), g82(rp_sh[:]), sinA, ALU.mult), sig="v")
                p.wait("vector", dst_gate, t_t1, t_sm)
                t_add = p.do("vector", lambda e: e.tensor_add(kq_bf[:], rp_t1[:], rp_sh[:]), sig="v")
                rp_free[0] = t_add
                return t_add

            sA1 = g.enter_context(ExitStack())
            wk_sb = sA1.enter_context(nc.sbuf_tensor("wk_sb", [128, 16 * 512], BF16))
            wv_sb = sA1.enter_context(nc.sbuf_tensor("wv_sb", [128, 16 * 512], BF16))
            xnT_kv = sA1.enter_context(nc.sbuf_tensor("xnT_kv", [128, 16 * 128], BF16))
            xn_bfs = [sA1.enter_context(nc.sbuf_tensor(f"xn_bf{i}", [128, 2048], BF16))
                      for i in range(2)]
            xbufs.append(sA1.enter_context(nc.sbuf_tensor("xbuf2", [128, 2048], F32)))
            pA = g.enter_context(ExitStack())
            tpP0 = pA.enter_context(nc.psum_tensor("tpP0", [128, 1024], F32))
            tpP1 = pA.enter_context(nc.psum_tensor("tpP1", [128, 1024], F32))
            tpPs = [tpP0, tpP1]
            kP = pA.enter_context(nc.psum_tensor("kP", [128, 1024], F32))
            vP = pA.enter_context(nc.psum_tensor("vP", [128, 1024], F32))

            t_wk = load(lambda: wk_sb[:].rearrange("p (i c) -> p i c", c=512),
                        lambda: wk.ap().rearrange("(i p) c -> p i c", p=128),
                        eng="gpsimd", sig="cw")
            t_wv = load(lambda: wv_sb[:].rearrange("p (i c) -> p i c", c=512),
                        lambda: wv.ap().rearrange("(i p) c -> p i c", p=128),
                        eng="gpsimd", sig="cw")

            xb_free = [None, None, None]
            stats_free = [None, None, None, None]
            tp_free = [None, None]
            k_free = [None, None]
            v_free = [None, None]
            xn_free = [None, None]
            kq_free = [None, None]
            rstds = [None] * N_A1
            xns = [None] * N_A1
            kvs = [None] * N_A1
            krs = [None] * N_A1
            evss = [None] * N_A1

            t_xs = [None] * N_A1

            def a1_load(t):
                if t >= N_A1:
                    return
                xi = t % 3
                xq = "sync" if t % 2 == 0 else "scalar"
                p.wait(xq, xb_free[xi])
                t_xs[t] = load(lambda xi=xi: xbufs[xi][:],
                               lambda t=t: x[t * 128:(t + 1) * 128, :],
                               eng=xq, sig="lx%d" % xi)

            def a1_front(t):
                xi = t % 3
                xb = xbufs[xi]
                sl = (t % 4) * 8
                rstds[t] = layernorm_stats(lambda xb=xb: xb[:], sl, (t_xs[t],),
                                           stats_free[t % 4])

            def a1_xn(t):
                xi = t % 3
                sl = (t % 4) * 8
                xnb = xn_bfs[t % 2]
                t_var, t_rstd = rstds[t]
                p.wait("vector", t_rstd, t_var, xn_free[t % 2])
                xns[t] = p.do("vector", lambda e, xi=xi, sl=sl, xnb=xnb: e.tensor_scalar(
                    xnb[:], xbufs[xi][:], stats[:, sl + 2:sl + 3], stats[:, sl + 6:sl + 7],
                    ALU.subtract, ALU.mult), sig="v")
                xb_free[xi] = xns[t]
                stats_free[t % 4] = xns[t]

            def a1_ktp(t):
                par = t % 2
                kqb = kq_bfs[par]
                tp = tpPs[par]
                p.wait("tensor", krs[t], tp_free[par])
                t_ktp = None
                for j in range(4):
                    t_ktp = p.do("tensor", lambda e, j=j, tp=tp, kqb=kqb: e.matmul(
                        tp[:, j * 128:(j + 1) * 128], kqb[:, j * 128:(j + 1) * 128],
                        id_bf[:], start=True, stop=True), sig="t" if j == 3 else None)
                kq_free[par] = t_ktp
                p.wait("vector", t_ktp)
                t_kev = p.do("vector", lambda e, t=t, tp=tp: e.tensor_copy(
                    bass.AP(KT, t * 128, [[4 * 4096, 128], [4096, 4], [1, 128]]),
                    tp[:, 0:512].rearrange("p (j c) -> p j c", c=128)), sig="v")
                tp_free[par] = t_kev

            def a1_mid(t):
                par = t % 2
                xnb = xn_bfs[par]
                own = t < 8
                if t >= 1:
                    a1_ktp(t - 1)
                t_tp_last = None
                evs = []
                for hb in range(2):
                    tp = tpPs[par]
                    p.wait("tensor", xns[t], tp_free[par], t_idb)
                    t_tp = None
                    for j in range(8):
                        ic = hb * 8 + j
                        t_tp = p.do("tensor", lambda e, ic=ic, j=j, tp=tp, xnb=xnb: e.matmul(
                            tp[:, j * 128:(j + 1) * 128], xnb[:, ic * 128:(ic + 1) * 128],
                            id_bf[:], start=True, stop=True), sig="t" if j == 7 else None)
                    t_tp_last = t_tp
                    p.wait("vector", t_tp)
                    if own:
                        def ev(e, hb=hb, t=t, tp=tp):
                            dst = bass.AP(xnT_own, (hb * 8) * 1024 + t * 128,
                                          [[16 * 1024, 128], [1024, 8], [1, 128]])
                            return e.tensor_copy(dst, tp[:].rearrange("p (i c) -> p i c", c=128))
                    else:
                        def ev(e, hb=hb, tp=tp):
                            dst = bass.AP(xnT_kv, (hb * 8) * 128,
                                          [[16 * 128, 128], [128, 8], [1, 128]])
                            return e.tensor_copy(dst, tp[:].rearrange("p (i c) -> p i c", c=128))
                    t_ev = p.do("vector", ev, sig="v")
                    tp_free[par] = t_ev
                    evs.append(t_ev)
                xn_free[par] = t_tp_last
                xnT = xnT_own if own else xnT_kv
                colw = 1024 if own else 128
                colo = t * 128 if own else 0
                p.wait("tensor", t_wk, t_wv, k_free[par], v_free[par])
                t_kv = None
                for half in range(2):
                    p.wait("tensor", evs[half])
                    for icl in range(8):
                        ic = half * 8 + icl
                        p.do("tensor", lambda e, ic=ic, par=par, xnT=xnT, colw=colw, colo=colo: e.matmul(
                            kP[:, par * 512:(par + 1) * 512],
                            xnT[:, ic * colw + colo: ic * colw + colo + 128],
                            wk_sb[:, ic * 512:(ic + 1) * 512],
                            start=(ic == 0), stop=(ic == 15)))
                        t_kv = p.do("tensor", lambda e, ic=ic, par=par, xnT=xnT, colw=colw, colo=colo: e.matmul(
                            vP[:, par * 512:(par + 1) * 512],
                            xnT[:, ic * colw + colo: ic * colw + colo + 128],
                            wv_sb[:, ic * 512:(ic + 1) * 512],
                            start=(ic == 0), stop=(ic == 15)), sig="t" if ic == 15 else None)
                kvs[t] = t_kv
                p.wait("vector", t_kv, t_vones)
                t_vev = p.do("vector", lambda e, t=t, par=par: e.tensor_copy(
                    bass.AP(VT, t * 520, [[32 * 520, 128], [65, 8], [1, 64]]),
                    vP[:, par * 512:(par + 1) * 512].rearrange("p (g c) -> p g c", g=8)), sig="v")
                v_free[par] = t_vev
                kqb = kq_bfs[par]
                krs[t] = rope(lambda par=par: kP[:, par * 512:(par + 1) * 512], t, (t_kv,),
                              kq_free[par], kqb)
                k_free[par] = krs[t]

            a1_load(0)
            a1_load(1)
            a1_load(2)
            for i in range(N_A1 + 1):
                if i < N_A1:
                    a1_front(i)
                if i - 1 >= 0:
                    a1_mid(i - 1)
                if i < N_A1:
                    a1_xn(i)
                    a1_load(i + 3)
            a1_ktp(N_A1 - 1)

            dbg_dump("KT", lambda: KT[:], p.barrier_tok("v"))
            dbg_dump("VT", lambda: VT[:], p.barrier_tok("v"))
            dbg_dump("xnT_own", lambda: xnT_own[:], p.barrier_tok("v"))
            p.flush()
            sA1.close()
            full_barrier()
            if BUILD_TO == "a1":
                finish(lambda: xbufs[0][:])
                return nc, dbg_outs

            QT = sKV.enter_context(nc.sbuf_tensor("QT", [128, 16 * 1024], BF16))
            wq_sbs = [sRope.enter_context(nc.sbuf_tensor(f"wq_sb{i}", [128, 16 * 512], BF16))
                      for i in range(2)]
            wq_free = [None, None]
            t_wqs = [None] * 4
            q_free = [k_free[0], k_free[1]]
            t_qs = [None] * 32
            t_qrs = [None] * 32

            def a2_wq_load(q):
                if q >= 4:
                    return
                p.wait("gpsimd", wq_free[q % 2])
                t_wqs[q] = load(
                    lambda q=q: wq_sbs[q % 2][:].rearrange("p (i c) -> p i c", c=512),
                    lambda q=q: wq.ap().rearrange("(i p) c -> p i c", p=128)[:, :, q * 512:(q + 1) * 512],
                    eng="gpsimd", sig="cq")

            def a2_mm(i):
                qtr, tl = i // 8, i % 8
                tpar = tl % 2
                p.wait("tensor", t_wqs[qtr], q_free[tpar])
                t_q = None
                for ic in range(16):
                    t_q = p.do("tensor", lambda e, ic=ic, tl=tl, tpar=tpar, qtr=qtr: e.matmul(
                        kP[:, tpar * 512:(tpar + 1) * 512],
                        xnT_own[:, ic * 1024 + tl * 128: ic * 1024 + tl * 128 + 128],
                        wq_sbs[qtr % 2][:, ic * 512:(ic + 1) * 512],
                        start=(ic == 0), stop=(ic == 15)), sig="t" if ic == 15 else None)
                t_qs[i] = t_q
                if tl == 7:
                    wq_free[qtr % 2] = t_q

            def a2_rope(i):
                tl = i % 8
                tpar = tl % 2
                kqb = kq_bfs[tpar]
                t_qr = rope(lambda tpar=tpar: kP[:, tpar * 512:(tpar + 1) * 512],
                            tl, (t_qs[i],), kq_free[tpar], kqb)
                t_qrs[i] = t_qr
                q_free[tpar] = t_qr

            def a2_qtp(i):
                qtr, tl = i // 8, i % 8
                tpar = tl % 2
                kqb = kq_bfs[tpar]
                tp = tpPs[tpar]
                p.wait("tensor", t_qrs[i], tp_free[tpar])
                t_qtp = None
                for m in range(4):
                    t_qtp = p.do("tensor", lambda e, m=m, tp=tp, kqb=kqb: e.matmul(
                        tp[:, m * 128:(m + 1) * 128], kqb[:, m * 128:(m + 1) * 128],
                        id_bf[:], start=True, stop=True), sig="t" if m == 3 else None)
                kq_free[tpar] = t_qtp
                p.wait("vector", t_qtp)
                t_qev = None
                for w in range(8):
                    m, sr = w // 2, 64 * (w % 2)
                    pi, dr = 4 * qtr + (w % 4), 64 * (w // 4)
                    t_qev = p.do("vector", lambda e, m=m, sr=sr, pi=pi, dr=dr, tl=tl, tp=tp:
                                 e.tensor_copy(
                                     QT[dr:dr + 64, pi * 1024 + tl * 128: pi * 1024 + tl * 128 + 128],
                                     tp[sr:sr + 64, m * 128:(m + 1) * 128]), sig="v" if w == 7 else None)
                tp_free[tpar] = t_qev

            a2_wq_load(0)
            a2_wq_load(1)
            for i in range(34):
                if i < 32:
                    if i % 8 == 0 and i >= 8:
                        a2_wq_load(i // 8 + 1)
                    a2_mm(i)
                if 0 <= i - 1 < 32:
                    a2_rope(i - 1)
                if 0 <= i - 2 < 32:
                    a2_qtp(i - 2)

            dbg_dump("QT", lambda: QT[:], p.barrier_tok("v"))
            p.flush()
            sRope.close()
            pA.close()
            full_barrier()
            if BUILD_TO == "a2":
                finish(lambda: xbufs[0][:])
                return nc, dbg_outs

            sAo = g.enter_context(ExitStack())
            aoT = sAo.enter_context(nc.sbuf_tensor("aoT", [128, 16 * 1024], BF16, side="right"))
            s3 = g.enter_context(ExitStack())
            f0b = [s3.enter_context(nc.sbuf_tensor(f"f0b{i}", [128, 4096], BF16)) for i in range(2)]
            f0_free = [None, None]
            for i in range(64):
                par = i % 2
                p.wait("gpsimd", f0_free[par])
                t_l = load(lambda par=par: f0b[par][:, 0:2048],
                           lambda i=i: upw[i * 128:(i + 1) * 128, :],
                           eng="gpsimd", sig="f0l%d" % par)
                p.wait("sync", t_l)
                f0_free[par] = load(lambda i=i: scr_up[i * 128:(i + 1) * 128, :],
                                    lambda par=par: f0b[par][:, 0:2048], sig="f0s%d" % par)
            for i in range(16):
                for hh in range(2):
                    par = (i * 2 + hh) % 2
                    p.wait("gpsimd", f0_free[par])
                    t_l = load(lambda par=par: f0b[par][:],
                               lambda i=i, hh=hh: dnw[i * 128:(i + 1) * 128,
                                                      hh * 4096:(hh + 1) * 4096],
                               eng="gpsimd", sig="f0l%d" % par)
                    p.wait("sync", t_l)
                    f0_free[par] = load(lambda i=i, hh=hh: scr_dn[i * 128:(i + 1) * 128,
                                                                 hh * 4096:(hh + 1) * 4096],
                                        lambda par=par: f0b[par][:], sig="f0s%d" % par)
            t_f0_done = [p.barrier_tok("f0s0"), p.barrier_tok("f0s1")]
            t_f0_done = [(s_, v_, "_none_", 16) for (s_, v_, _, _) in t_f0_done]
            expA = s3.enter_context(nc.sbuf_tensor("expA", [128, 2 * 1024], BF16))
            expB = s3.enter_context(nc.sbuf_tensor("expB", [128, 2 * 1024], BF16))
            r_sb = [s3.enter_context(nc.sbuf_tensor(f"r_sb{i}", [1, 1024], F32)) for i in range(2)]
            rbc_sb = [s3.enter_context(nc.sbuf_tensor(f"rbc_sb{i}", [64, 1024], F32)) for i in range(2)]
            pvs_sb = [s3.enter_context(nc.sbuf_tensor(f"pvs_sb{i}", [65, 1024], F32)) for i in range(2)]
            p3 = g.enter_context(ExitStack())
            scA = p3.enter_context(nc.psum_tensor("scA", [128, 1024], F32))
            scB = p3.enter_context(nc.psum_tensor("scB", [128, 1024], F32))
            pvA = p3.enter_context(nc.psum_tensor("pvA", [128, 1024], F32))
            pvB = p3.enter_context(nc.psum_tensor("pvB", [128, 1024], F32))

            t_ready = p.barrier_tok("v")
            sc = {0: scA, 1: scB}
            pv = {0: pvA, 1: pvB}
            ex = {0: expA, 1: expB}
            sc_free = {0: t_ready, 1: t_ready}
            pv_free = {0: None, 1: None}
            fin_free = {0: None, 1: None}
            pend = [None]

            def flush_fin():
                if pend[0] is None:
                    return
                heads_, t_rs = pend[0]
                pend[0] = None
                for hx in (0, 1):
                    h = heads_[hx]
                    p.wait("tensor", t_rs[hx], t_ones)
                    t_rb = None
                    for s_ in range(2):
                        t_rb = p.do("tensor", lambda e, s_=s_, hx=hx: e.matmul(
                            pv[hx][0:64, s_ * 512:(s_ + 1) * 512], ones_f[:, 0:64],
                            r_sb[hx][:, s_ * 512:(s_ + 1) * 512], start=True, stop=True),
                            sig="t" if s_ == 1 else None)
                    p.wait("vector", t_rb)
                    t_rc = p.do("vector", lambda e, hx=hx: e.tensor_copy(
                        rbc_sb[hx][:], pv[hx][0:64, :]), sig="v")
                    p.wait("vector", t_rc)
                    t_fin = p.do("vector", lambda e, hx=hx, h=h: e.tensor_tensor(
                        aoT[64 * (h % 2):64 * (h % 2) + 64, (h // 2) * 1024:(h // 2 + 1) * 1024],
                        pvs_sb[hx][0:64, :], rbc_sb[hx][:], ALU.mult), sig="v")
                    pv_free[hx] = t_rc
                    fin_free[hx] = t_fin
            exp_free = {0: [None, None], 1: [None, None]}
            r_free = None
            t_fin = None

            pairs = []
            for pp in range(4):
                for j in range(4):
                    pairs.append((8 * pp + j, 8 * pp + 4 + j))

            def emit_pv(pi, heads, kc, hx, t_exp_kc):
                g_ = heads[hx] // 4
                p.wait("tensor", t_exp_kc)
                if kc == 0:
                    p.wait("tensor", pv_free[hx])
                t_pv = None
                for s_ in range(2):
                    t_pv = p.do("tensor", lambda e, kc=kc, s_=s_, hx=hx, g_=g_: e.matmul(
                        pv[hx][0:65, s_ * 512:(s_ + 1) * 512],
                        VT[:, kc * 520 + g_ * 65: kc * 520 + g_ * 65 + 65],
                        ex[hx][:, (kc % 2) * 1024 + s_ * 512:(kc % 2) * 1024 + (s_ + 1) * 512],
                        start=(kc == 0), stop=(kc == 31)),
                        sig="t" if s_ == 1 else None)
                exp_free[hx][kc % 2] = t_pv
                return t_pv

            for pi, (hA, hB) in enumerate(pairs):
                heads = {0: hA, 1: hB}
                t_exp = {0: None, 1: None}
                t_exp_prev = {0: None, 1: None}
                t_pv = {0: None, 1: None}
                for kc in range(32):
                    for hx in (0, 1):
                        h = heads[hx]
                        gidx = h // 4
                        ptile = gidx // 2
                        rbase = 64 * (gidx % 2)
                        p.wait("tensor", sc_free[hx])
                        tq = None
                        for s_ in range(2):
                            tq = p.do("tensor", lambda e, kc=kc, s_=s_, hx=hx, ptile=ptile,
                                      rbase=rbase, pi=pi: e.matmul(
                                sc[hx][:, s_ * 512:(s_ + 1) * 512],
                                KT[rbase:rbase + 64,
                                   ptile * 4096 + kc * 128: ptile * 4096 + kc * 128 + 128],
                                QT[rbase:rbase + 64,
                                   pi * 1024 + s_ * 512: pi * 1024 + (s_ + 1) * 512],
                                start=True, stop=True), sig="t" if s_ == 1 else None)
                        p.wait("scalar", tq, t_eb, exp_free[hx][kc % 2])
                        t_exp[hx] = p.do("scalar", lambda e, hx=hx, kc=kc: e.activation(
                            ex[hx][:, (kc % 2) * 1024:(kc % 2 + 1) * 1024], sc[hx][:],
                            AF.Exp, bias=ebias[:], scale=0.125), sig="s")
                        sc_free[hx] = t_exp[hx]
                    if kc == 0:
                        flush_fin()
                    if kc >= 1:
                        for hx in (0, 1):
                            t_pv[hx] = emit_pv(pi, heads, kc - 1, hx, t_exp_prev[hx])
                    t_exp_prev = dict(t_exp)
                for hx in (0, 1):
                    t_pv[hx] = emit_pv(pi, heads, 31, hx, t_exp_prev[hx])
                t_rs = {}
                for hx in (0, 1):
                    p.wait("vector", t_pv[hx], fin_free[hx])
                    t_pvs = p.do("vector", lambda e, hx=hx: e.tensor_copy(
                        pvs_sb[hx][:], pv[hx][0:65, :]), sig="v")
                    pv_free[hx] = t_pvs
                for hx in (0, 1):
                    p.wait("vector", pv_free[hx])
                    t_rs[hx] = p.do("vector", lambda e, hx=hx: e.reciprocal(
                        r_sb[hx][:], pvs_sb[hx][64:65, 0:1024]), sig="v")
                pend[0] = (dict(heads), t_rs)

            flush_fin()
            dbg_dump("aoT", lambda: aoT[:], p.barrier_tok("v"))
            p.flush()
            s3.close()
            p3.close()
            sKV.close()
            full_barrier()
            if BUILD_TO == "a3":
                finish(lambda: xbufs[0][:])
                return nc, dbg_outs

            s4 = g.enter_context(ExitStack())
            wo_sb = s4.enter_context(nc.sbuf_tensor("wo_sb", [128, 16 * 2048], BF16, side="right"))
            xr2t = [s4.enter_context(nc.sbuf_tensor(f"xr2t{i}", [128, 2048], F32, side="right")) for i in range(2)]
            xn2_bf = s4.enter_context(nc.sbuf_tensor("xn2_bf", [128, 2048], BF16, side="right"))
            aps = s4.enter_context(nc.sbuf_tensor("aps", [128, 2048], F32, side="right"))
            xn2T = g.enter_context(ExitStack())
            xn2T_sb = xn2T.enter_context(nc.sbuf_tensor("xn2T", [128, 16 * 1024], BF16))
            p4 = g.enter_context(ExitStack())
            woP0 = p4.enter_context(nc.psum_tensor("woP0", [128, 2048], F32))
            woP1 = p4.enter_context(nc.psum_tensor("woP1", [128, 2048], F32))
            woPs = [woP0, woP1]

            t_wo = load(lambda: wo_sb[:].rearrange("p (i c) -> p i c", c=2048),
                        lambda: wo.ap().rearrange("(i p) c -> p i c", p=128),
                        eng="gpsimd", sig="co")
            t_aoT_ready = p.barrier_tok("v")
            if A4_STAGE == 1:
                p.wait("sync", t_wo)
                finish(lambda: xbufs[0][:])
                return nc, dbg_outs

            wo_free = [None, None]
            xb_free4 = [xb_free[0], xb_free[1]]
            stats_free4 = [stats_free[0], stats_free[1]]
            xr2_free = [None, None]
            xn2_free = None
            t_spill = [None] * 8
            t_x2ev = None
            for tl in range(8):
                par = tl % 2
                woP = woPs[par]
                xb = xbufs[par]
                xrt = xr2t[par]
                sl = par * 8
                p.wait("sync", xb_free4[par])
                t_x = load(lambda xb=xb: xb[:], lambda tl=tl: x[tl * 128:(tl + 1) * 128, :],
                           sig="lx%d" % par)
                p.wait("tensor", t_wo, t_aoT_ready, wo_free[par])
                t_mm = None
                for ic in range(16):
                    for s_ in range(4):
                        t_mm = p.do("tensor", lambda e, ic=ic, s_=s_, tl=tl, woP=woP: e.matmul(
                            woP[:, s_ * 512:(s_ + 1) * 512],
                            aoT[:, ic * 1024 + tl * 128: ic * 1024 + tl * 128 + 128],
                            wo_sb[:, ic * 2048 + s_ * 512: ic * 2048 + (s_ + 1) * 512],
                            start=(ic == 0), stop=(ic == 15)),
                            sig="t" if (ic == 15 and s_ == 3) else None)
                t_var, t_rstd = layernorm_stats(lambda woP=woP: woP[:], sl, (t_mm,), stats_free4[par], psum_in=True)
                p.wait("vector", t_rstd, t_var)
                t_ap = p.do("vector", lambda e, woP=woP, sl=sl: e.tensor_scalar(
                    aps[:], woP[:], stats[:, sl + 2:sl + 3], stats[:, sl + 6:sl + 7],
                    ALU.subtract, ALU.mult), sig="v")
                p.wait("vector", t_x, t_ap, xr2_free[par], t_spill[tl - 2] if tl >= 2 else None)
                t_xr2 = p.do("vector", lambda e, xrt=xrt, xb=xb: e.tensor_add(
                    xrt[:], aps[:], xb[:]), sig="v")
                xb_free4[par] = t_xr2
                if A4_STAGE == 2:
                    xb_free4[par] = t_xr2
                    stats_free4[par] = t_ap
                    xr2_free[par] = t_xr2
                    wo_free[par] = t_ap
                    t_spill[tl] = None
                    continue
                p.wait("sync", t_xr2)
                t_spill[tl] = load(lambda tl=tl: xr2sp[tl * 128:(tl + 1) * 128, :],
                                   lambda xrt=xrt: xrt[:], sig="sp%d" % par)
                t_var2, t_rstd2 = layernorm_stats(lambda xrt=xrt: xrt[:], sl, (t_xr2,), t_ap)
                p.wait("vector", t_rstd2, t_var2, xn2_free)
                t_xn2 = p.do("vector", lambda e, xrt=xrt, sl=sl: e.tensor_scalar(
                    xn2_bf[:], xrt[:], stats[:, sl + 2:sl + 3], stats[:, sl + 6:sl + 7],
                    ALU.subtract, ALU.mult), sig="v")
                stats_free4[par] = t_xn2
                xr2_free[par] = t_xn2
                t_tp_last = None
                tp_gate = t_ap
                for hb in range(2):
                    p.wait("tensor", t_xn2, tp_gate)
                    t_tp = None
                    for j in range(8):
                        ic = hb * 8 + j
                        t_tp = p.do("tensor", lambda e, ic=ic, j=j, woP=woP: e.matmul(
                            woP[:, j * 128:(j + 1) * 128], xn2_bf[:, ic * 128:(ic + 1) * 128],
                            id_bf[:], start=True, stop=True), sig="t" if j == 7 else None)
                    t_tp_last = t_tp
                    p.wait("vector", t_tp)
                    t_x2ev = p.do("vector", lambda e, hb=hb, tl=tl, woP=woP: e.tensor_copy(
                        bass.AP(xn2T_sb, (hb * 8) * 1024 + tl * 128,
                                [[16 * 1024, 128], [1024, 8], [1, 128]]),
                        woP[:, 0:1024].rearrange("p (i c) -> p i c", c=128)), sig="v")
                    tp_gate = t_x2ev
                xn2_free = t_tp_last
                wo_free[par] = t_x2ev

            dbg_dump("xr2", lambda: xr2t[1][:], p.barrier_tok("v"))
            dbg_dump("xn2T", lambda: xn2T_sb[:], p.barrier_tok("v"))
            p.flush()
            s4.close()
            sAo.close()
            p4.close()
            full_barrier()
            if BUILD_TO == "a4":
                finish(lambda: xbufs[0][:])
                return nc, dbg_outs

            dacc = g.enter_context(ExitStack())
            dacc_sb = dacc.enter_context(nc.sbuf_tensor("dacc", [128, 16 * 1024], F32, side="right"))
            sF = g.enter_context(ExitStack())
            hid = sF.enter_context(nc.sbuf_tensor("hid", [128, 16 * 1024], BF16))
            pan = [sF.enter_context(nc.sbuf_tensor(f"pan{i}", [128, 16 * 512], BF16)) for i in range(2)]
            pF = g.enter_context(ExitStack())
            accP = pF.enter_context(nc.psum_tensor("accP", [128, 2048], F32))
            tpF = pF.enter_context(nc.psum_tensor("tpF", [128, 2048], F32))

            t_xn2T_ready = p.barrier_tok("v")
            p.wait("sync", *t_f0_done)
            pan_free = [None, None]
            pan_sg = [None, None]
            acc_free = [None, None]
            hid_free = None
            t_dacc = [None] * 16
            t_dm = None

            for qt in range(4):
                t_hid = [None] * 16
                for pn in range(4):
                    par = pn % 2
                    p.wait("sync", pan_free[par], pan_sg[par])
                    t_pl = None
                    for ic in range(16):
                        t_pl = p.do("sync", lambda e, ic=ic, pn=pn, qt=qt, par=par: e.dma_start(
                            out=pan[par][:, ic * 512:(ic + 1) * 512],
                            in_=scr_up[qt * 2048 + pn * 512: qt * 2048 + (pn + 1) * 512,
                                       ic * 128:(ic + 1) * 128],
                            transpose=True), sig="tu%d" % par, inc=16)
                    p.wait("scalar", t_pl)
                    t_sg = p.do("scalar", lambda e, par=par: e.activation(
                        pan[par][:], pan[par][:], AF.Sign), sig="s")
                    pan_sg[par] = t_sg
                    t_um = None
                    for hcl in range(4):
                        hc = pn * 4 + hcl
                        hq = qt * 16 + hc
                        apar = hc % 2
                        p.wait("tensor", t_sg, acc_free[apar], t_xn2T_ready)
                        for ic in range(16):
                            for s_ in range(2):
                                t_um = p.do("tensor", lambda e, ic=ic, s_=s_, apar=apar,
                                            par=par, hcl=hcl: e.matmul(
                                    accP[:, apar * 1024 + s_ * 512: apar * 1024 + (s_ + 1) * 512],
                                    pan[par][:, ic * 512 + hcl * 128: ic * 512 + hcl * 128 + 128],
                                    xn2T_sb[:, ic * 1024 + s_ * 512: ic * 1024 + (s_ + 1) * 512],
                                    start=(ic == 0), stop=(ic == 15)),
                                    sig="t" if (ic == 15 and s_ == 1) else None)
                        p.wait("vector", t_um, t_upsl, hid_free)
                        t_hid[hc] = p.do("vector", lambda e, hc=hc, apar=apar, hq=hq: e.tensor_scalar(
                            hid[:, hc * 1024:(hc + 1) * 1024],
                            accP[:, apar * 1024:(apar + 1) * 1024],
                            ups_sb[:, hq:hq + 1], 0.0, ALU.mult, ALU.max), sig="v")
                        acc_free[apar] = t_hid[hc]
                    pan_free[par] = t_um
                t_hid_all = t_hid[15]
                for pn in range(4):
                    par = pn % 2
                    p.wait("sync", pan_free[par], pan_sg[par])
                    t_pl = None
                    for j in range(16):
                        t_pl = p.do("sync", lambda e, j=j, pn=pn, qt=qt, par=par: e.dma_start(
                            out=pan[par][:, j * 512:(j + 1) * 512],
                            in_=scr_dn[pn * 512:(pn + 1) * 512,
                                       (qt * 16 + j) * 128:(qt * 16 + j + 1) * 128],
                            transpose=True), sig="tu%d" % par, inc=16)
                    p.wait("scalar", t_pl)
                    t_sg = p.do("scalar", lambda e, par=par: e.activation(
                        pan[par][:], pan[par][:], AF.Sign), sig="s")
                    pan_sg[par] = t_sg
                    t_dm = None
                    for dcl in range(4):
                        dc = pn * 4 + dcl
                        apar = dc % 2
                        p.wait("tensor", t_sg, t_hid_all, acc_free[apar])
                        for j in range(16):
                            for s_ in range(2):
                                t_dm = p.do("tensor", lambda e, j=j, s_=s_, apar=apar,
                                            par=par, dcl=dcl: e.matmul(
                                    accP[:, apar * 1024 + s_ * 512: apar * 1024 + (s_ + 1) * 512],
                                    pan[par][:, j * 512 + dcl * 128: j * 512 + dcl * 128 + 128],
                                    hid[:, j * 1024 + s_ * 512: j * 1024 + (s_ + 1) * 512],
                                    start=(j == 0), stop=(j == 15)),
                                    sig="t" if (j == 15 and s_ == 1) else None)
                        p.wait("vector", t_dm)
                        if qt == 0:
                            t_dacc[dc] = p.do("vector", lambda e, dc=dc, apar=apar: e.tensor_copy(
                                dacc_sb[:, dc * 1024:(dc + 1) * 1024],
                                accP[:, apar * 1024:(apar + 1) * 1024]), sig="v")
                        else:
                            p.wait("vector", t_dacc[dc])
                            t_dacc[dc] = p.do("vector", lambda e, dc=dc, apar=apar: e.tensor_add(
                                dacc_sb[:, dc * 1024:(dc + 1) * 1024],
                                accP[:, apar * 1024:(apar + 1) * 1024],
                                dacc_sb[:, dc * 1024:(dc + 1) * 1024]), sig="v")
                        acc_free[apar] = t_dacc[dc]
                    pan_free[par] = t_dm
                hid_free = t_dm

            dbg_dump("dacc", lambda: dacc_sb[:], p.barrier_tok("v"))
            p.flush()
            sF.close()
            xn2T.close()
            full_barrier()
            if BUILD_TO == "ffn":
                finish(lambda: xbufs[0][:])
                pF.close()
                dacc.close()
                return nc, dbg_outs

            sfin = dacc.enter_context(nc.sbuf_tensor("sfin", [128, 16], F32, side="right"))
            outb = [dacc.enter_context(nc.sbuf_tensor(f"outb{i}", [128, 2048], F32, side="right")) for i in range(2)]
            aps = dacc.enter_context(nc.sbuf_tensor("apsF", [128, 2048], F32, side="right"))
            p.wait("tensor", t_outsl, t_ones, p.barrier_tok("v"))
            t_os = p.do("tensor", lambda e: e.matmul(
                accP[:, 0:1], ones_f[:, 0:128], outs_sb[:], start=True, stop=True), sig="t")
            p.wait("vector", t_os, t_dnsl)
            t_sf = p.do("vector", lambda e: e.tensor_scalar(
                sfin[:], dns_sb[:], accP[:, 0:1], None, ALU.mult), sig="v")
            t_dsc = [None] * 16
            for dc in range(16):
                p.wait("vector", t_sf, t_dacc[dc])
                t_dsc[dc] = p.do("vector", lambda e, dc=dc: e.tensor_scalar(
                    dacc_sb[:, dc * 1024:(dc + 1) * 1024], dacc_sb[:, dc * 1024:(dc + 1) * 1024],
                    sfin[:, dc:dc + 1], None, ALU.mult), sig="v")

            stats_free6 = [stats_free4[0], stats_free4[1]]
            xb_free6 = [xb_free4[0], xb_free4[1]]
            ftp_free = [None, None]
            aps_free = None
            out_done = [None] * 8
            for tl in range(8):
                par = tl % 2
                sl = par * 8
                xb = xbufs[par]
                p.wait("sync", xb_free6[par], t_spill[tl])
                t_x2 = load(lambda xb=xb: xb[:], lambda tl=tl: xr2sp[tl * 128:(tl + 1) * 128, :],
                            sig="lx%d" % par)
                t_fev = None
                for hb in range(4):
                    p.wait("tensor", ftp_free[par], t_idf, t_os)
                    t_tp = None
                    for j in range(4):
                        dc = hb * 4 + j
                        p.wait("tensor", t_dsc[dc])
                        t_tp = p.do("tensor", lambda e, dc=dc, j=j, par=par, tl=tl: e.transpose(
                            tpF[:, par * 1024 + j * 128: par * 1024 + j * 128 + 128],
                            dacc_sb[:, dc * 1024 + tl * 128: dc * 1024 + tl * 128 + 128],
                            id_f[:]), sig="t" if j == 3 else None)
                    p.wait("vector", t_tp, aps_free)
                    t_fev = p.do("vector", lambda e, hb=hb, par=par: e.tensor_copy(
                        aps[:, hb * 512:(hb + 1) * 512],
                        tpF[:, par * 1024: par * 1024 + 512]), sig="v")
                    ftp_free[par] = t_fev
                t_var, t_rstd = layernorm_stats(lambda: aps[:], sl, (t_fev,), stats_free6[par])
                p.wait("vector", t_rstd, t_var, t_x2, out_done[tl - 2] if tl >= 2 else None)
                t_o = p.do("vector", lambda e, par=par, sl=sl: e.tensor_scalar(
                    outb[par][:], aps[:], stats[:, sl + 2:sl + 3], stats[:, sl + 6:sl + 7],
                    ALU.subtract, ALU.mult), sig="v")
                aps_free = t_o
                p.wait("vector", t_o)
                t_o2 = p.do("vector", lambda e, par=par, xb=xb: e.tensor_add(
                    outb[par][:], outb[par][:], xb[:]), sig="v")
                stats_free6[par] = t_o2
                xb_free6[par] = t_o2
                p.wait("sync", t_o2)
                out_done[tl] = p.do("sync", lambda e, tl=tl, par=par: e.dma_start(
                    out=out[tl * 128:(tl + 1) * 128, :], in_=outb[par][:]), sig="so%d" % par, inc=16)

            all_dma_barrier("sync")
            p.flush()
            pF.close()
            dacc.close()
    return nc, dbg_outs


def _tables(qc):
    pos = (np.arange(4096) + qc * 1024) % 4096
    f = np.arange(32)
    inv = 10000.0 ** (-2.0 * f / 64.0)
    ang = pos[:, None] * inv[None, :]
    cos = np.cos(ang).astype(np.float32)
    sin = np.sin(ang).astype(np.float32)
    cosk = np.zeros((128, 1024), np.float32)
    sink = np.zeros((128, 1024), np.float32)
    for t in range(32):
        cosk[:, t * 32:(t + 1) * 32] = cos[t * 128:(t + 1) * 128]
        sink[:, t * 32:(t + 1) * 32] = sin[t * 128:(t + 1) * 128]
    return cosk, sink


def make_in_maps(L0_attn_x, L0_attn_wq, L0_attn_wk, L0_attn_wv, L0_attn_wo,
                 L0_ffn_up_weight, L0_ffn_down_weight, L0_ffn_up_scale,
                 L0_ffn_down_scale, L0_ffn_output_scale):
    in_maps = []
    for c in range(8):
        b, qc = c // 4, c % 4
        xr = np.roll(np.asarray(L0_attn_x[b]), -qc * 1024, axis=0)
        ck, sk = _tables(qc)
        in_maps.append({
            "x": np.ascontiguousarray(xr),
            "wq": np.asarray(L0_attn_wq), "wk": np.asarray(L0_attn_wk),
            "wv": np.asarray(L0_attn_wv), "wo": np.asarray(L0_attn_wo),
            "upw": np.asarray(L0_ffn_up_weight), "dnw": np.asarray(L0_ffn_down_weight),
            "ups": np.ascontiguousarray(np.asarray(L0_ffn_up_scale).reshape(64, 128).T),
            "dns": np.ascontiguousarray(np.asarray(L0_ffn_down_scale).reshape(16, 128).T),
            "outsc": np.asarray(L0_ffn_output_scale).reshape(1, 1),
            "cosk": ck, "sink": sk,
            "ident": np.eye(128, dtype=np.float32),
            "onesr": np.ones((1, 128), np.float32),
        })
    return in_maps


def kernel(**inputs):
    nc, _ = build_nc()
    in_maps = make_in_maps(**inputs)
    res = run_bass_kernel_spmd(nc, in_maps, core_ids=list(range(8)))
    outp = np.empty((2, 4096, 2048), np.float32)
    for c in range(8):
        b, qc = c // 4, c % 4
        outp[b, qc * 1024:(qc + 1) * 1024] = res.results[c]["out"]
    return outp


# version 13
# speedup vs baseline: 1.0307x; 1.0307x over previous
import numpy as np
from contextlib import ExitStack

import concourse.bass as bass
import concourse.mybir as mybir
from concourse.bass_utils import run_bass_kernel_spmd

F32 = mybir.dt.float32
BF16 = mybir.dt.bfloat16
AF = mybir.ActivationFunctionType
ALU = mybir.AluOpType
AX = mybir.AxisListType

ENGINES = ("sync", "gpsimd", "tensor", "vector", "scalar")

BUILD_TO = "all"
N_A1 = 32
A4_STAGE = 3
DBG = {}


class P:

    def __init__(self, nc, block):
        self.nc = nc
        self.block = block
        self.buf = {e: [] for e in ENGINES}
        self.cnt = {}
        self.sems = {}
        self.hwm = {e: {} for e in ENGINES}

    def sem(self, name, stack):
        h = stack.enter_context(self.nc.semaphore(name))
        self.sems[name] = h
        self.cnt[name] = 0

    SERIAL = False

    def do(self, eng, fn, sig=None, inc=1):
        if P.SERIAL:
            for s in list(self.cnt):
                self.wait(eng, (s, self.cnt[s], "_none_", 2))
        if sig is None:
            self.buf[eng].append(("op", fn, None, 0))
            return None
        self.cnt[sig] += inc
        val = self.cnt[sig]
        self.buf[eng].append(("op", fn, sig, inc))
        return (sig, val, eng, inc)

    def wait(self, eng, *toks):
        for tok in toks:
            if tok is None:
                continue
            sig, val, src, inc = tok
            if inc == 16:
                val = self.cnt[sig]
            if self.hwm[eng].get(sig, 0) >= val:
                continue
            self.hwm[eng][sig] = val
            self.buf[eng].append(("wait", sig, val))

    def barrier_tok(self, sig):
        return (sig, self.cnt[sig], "_none_", 2)

    def flush(self):
        for eng in ENGINES:
            items = self.buf[eng]
            if not items:
                continue
            self.buf[eng] = []
            dec = getattr(self.block, eng)
            sems = self.sems

            def body(e, items=items):
                for it in items:
                    if it[0] == "op":
                        _, fn, sig, inc = it
                        ins = fn(e)
                        if sig is not None:
                            ins.then_inc(sems[sig], inc)
                    else:
                        _, sig, val = it
                        e.wait_ge(sems[sig], val)
            dec(body)


def build_nc():
    nc = bass.Bass("TRN2", target_bir_lowering=False)

    x = nc.dram_tensor("x", [4096, 2048], F32, kind="ExternalInput")
    wq = nc.dram_tensor("wq", [2048, 2048], F32, kind="ExternalInput")
    wk = nc.dram_tensor("wk", [2048, 512], F32, kind="ExternalInput")
    wv = nc.dram_tensor("wv", [2048, 512], F32, kind="ExternalInput")
    wo = nc.dram_tensor("wo", [2048, 2048], F32, kind="ExternalInput")
    upw = nc.dram_tensor("upw", [8192, 2048], F32, kind="ExternalInput")
    dnw = nc.dram_tensor("dnw", [2048, 8192], F32, kind="ExternalInput")
    ups = nc.dram_tensor("ups", [128, 64], F32, kind="ExternalInput")
    dns = nc.dram_tensor("dns", [128, 16], F32, kind="ExternalInput")
    outsc = nc.dram_tensor("outsc", [1, 1], F32, kind="ExternalInput")
    cosk = nc.dram_tensor("cosk", [128, 1024], F32, kind="ExternalInput")
    sink = nc.dram_tensor("sink", [128, 1024], F32, kind="ExternalInput")
    ident = nc.dram_tensor("ident", [128, 128], F32, kind="ExternalInput")
    onesr = nc.dram_tensor("onesr", [1, 128], F32, kind="ExternalInput")
    out = nc.dram_tensor("out", [1024, 2048], F32, kind="ExternalOutput")
    xr2sp = nc.dram_tensor("xr2sp", [1024, 2048], F32, kind="Internal")
    scr_up = nc.dram_tensor("scr_up", [8192, 2048], BF16, kind="Internal")
    r_dram = [nc.dram_tensor(f"r_dram{i}", [1, 1024], F32, kind="Internal") for i in range(2)]
    scr_dn = nc.dram_tensor("scr_dn", [2048, 8192], BF16, kind="Internal")

    dbg_outs = {}
    for name, (shape, dt) in DBG.items():
        dbg_outs[name] = nc.dram_tensor("dbg_" + name, list(shape), dt, kind="ExternalOutput")

    g = ExitStack()
    with g:
        with nc.Block() as block:
            p = P(nc, block)
            DMA_SEMS = ("lc", "lx0", "lx1", "cw", "cq", "co", "cu0", "cu1",
                        "cd0", "cd1", "sp0", "sp1", "so0", "so1", "dbg", "lx2",
                        "f0l0", "f0l1", "f0s0", "f0s1", "tu0", "tu1", "rb0", "rb1")
            for s in DMA_SEMS + ("t", "v", "s"):
                p.sem(s, g)

            def all_dma_barrier(eng):
                for s in DMA_SEMS:
                    if p.cnt[s]:
                        p.wait(eng, (s, p.cnt[s], "_none_", 16))

            def full_barrier():
                for eng in ENGINES:
                    for s_ in ("t", "v", "s"):
                        if p.cnt[s_]:
                            p.wait(eng, (s_, p.cnt[s_], "_none_", 2))
                    all_dma_barrier(eng)

            def dbg_dump(name, ap_fn, *deps):
                if name not in DBG:
                    return
                p.wait("sync", *deps)
                p.do("sync", lambda e, f=ap_fn, d=dbg_outs[name]: e.dma_start(out=d[:], in_=f()),
                     sig="dbg", inc=16)

            def load(dst_ap_fn, src_ap_fn, *, eng="sync", sig="lc"):
                return p.do(eng, lambda e: e.dma_start(out=dst_ap_fn(), in_=src_ap_fn()),
                            sig=sig, inc=16)

            def finish(src_fn):
                for tl in range(8):
                    p.do("sync", lambda e, tl=tl: e.dma_start(
                        out=out[tl * 128:(tl + 1) * 128, :], in_=src_fn()), sig="so0", inc=16)
                all_dma_barrier("sync")
                p.flush()

            id_bf = g.enter_context(nc.sbuf_tensor("id_bf", [128, 128], BF16))
            id_f = g.enter_context(nc.sbuf_tensor("id_f", [128, 128], F32))
            ones_f = g.enter_context(nc.sbuf_tensor("ones_f", [1, 128], F32))
            ups_sb = g.enter_context(nc.sbuf_tensor("ups_sb", [128, 64], F32))
            dns_sb = g.enter_context(nc.sbuf_tensor("dns_sb", [128, 16], F32))
            outs_sb = g.enter_context(nc.sbuf_tensor("outs_sb", [1, 1], F32))
            ebias = g.enter_context(nc.sbuf_tensor("ebias", [128, 1], F32))
            epsb = g.enter_context(nc.sbuf_tensor("epsb", [128, 1], F32))
            stats = g.enter_context(nc.sbuf_tensor("stats", [128, 32], F32))
            sqscr = g.enter_context(nc.sbuf_tensor("sqscr", [128, 2048], F32))
            xbufs = [g.enter_context(nc.sbuf_tensor(f"xbuf{i}", [128, 2048], F32))
                     for i in range(2)]

            t_idb = load(lambda: id_bf[:], lambda: ident[:], eng="gpsimd", sig="cw")
            t_idf = load(lambda: id_f[:], lambda: ident[:])
            t_ones = load(lambda: ones_f[:], lambda: onesr[:])
            t_upsl = load(lambda: ups_sb[:], lambda: ups[:])
            t_dnsl = load(lambda: dns_sb[:], lambda: dns[:])
            t_outsl = load(lambda: outs_sb[:], lambda: outsc[:])
            t_eb = p.do("vector", lambda e: e.memset(ebias[:], -20.0), sig="v")
            t_eps = p.do("vector", lambda e: e.memset(epsb[:], 1e-5), sig="v")

            def layernorm_stats(ap_fn, sl, deps, stats_gate, psum_in=False):
                p.wait("scalar", *(list(deps) + [stats_gate]))
                t_ssq = p.do("scalar", lambda e: e.activation(
                    sqscr[:], ap_fn(), AF.Square, accum_out=stats[:, sl + 1:sl + 2]), sig="s")
                p.wait("vector", *(list(deps) + [stats_gate]))
                if psum_in:
                    p.wait("vector", t_ssq)
                t_sum = p.do("vector", lambda e: e.reduce_sum(
                    stats[:, sl:sl + 1], ap_fn(), axis=AX.X), sig="v")
                p.wait("vector", t_sum)
                t_mean = p.do("vector", lambda e: e.tensor_scalar(
                    stats[:, sl + 2:sl + 3], stats[:, sl:sl + 1], 1.0 / 2048, None, ALU.mult), sig="v")
                p.wait("vector", t_mean)
                t_msq = p.do("vector", lambda e: e.tensor_tensor(
                    stats[:, sl + 3:sl + 4], stats[:, sl + 2:sl + 3], stats[:, sl + 2:sl + 3], ALU.mult), sig="v")
                p.wait("vector", t_msq, t_ssq)
                t_var = p.do("vector", lambda e: e.scalar_tensor_tensor(
                    stats[:, sl + 4:sl + 5], stats[:, sl + 1:sl + 2], 1.0 / 2048,
                    stats[:, sl + 3:sl + 4], ALU.mult, ALU.subtract), sig="v")
                p.wait("scalar", t_var, t_eps)
                t_ln = p.do("scalar", lambda e: e.activation(
                    stats[:, sl + 5:sl + 6], stats[:, sl + 4:sl + 5], AF.Ln, bias=epsb[:]), sig="s")
                p.wait("scalar", t_ln)
                t_rstd = p.do("scalar", lambda e: e.activation(
                    stats[:, sl + 6:sl + 7], stats[:, sl + 5:sl + 6], AF.Exp, scale=-0.5), sig="s")
                return t_var, t_rstd

            sKV = g.enter_context(ExitStack())
            KT = sKV.enter_context(nc.sbuf_tensor("KT", [128, 4 * 4096], BF16))
            VT = sKV.enter_context(nc.sbuf_tensor("VT", [128, 32 * 520], BF16))

            sRope = g.enter_context(ExitStack())
            cos_sb = sRope.enter_context(nc.sbuf_tensor("cos_sb", [128, 1024], F32, side="right"))
            sin_sb = sRope.enter_context(nc.sbuf_tensor("sin_sb", [128, 1024], F32, side="right"))
            kq_bfs = [sRope.enter_context(nc.sbuf_tensor(f"kq_bf{i}", [128, 512], BF16, side="right"))
                      for i in range(2)]
            rp_sh = sRope.enter_context(nc.sbuf_tensor("rp_sh", [128, 512], F32, side="right"))
            rp_t1 = sRope.enter_context(nc.sbuf_tensor("rp_t1", [128, 512], F32, side="right"))
            xnT_own = sRope.enter_context(nc.sbuf_tensor("xnT_own", [128, 16 * 1024], BF16, side="right"))

            t_cos = load(lambda: cos_sb[:], lambda: cosk[:])
            t_sin = load(lambda: sin_sb[:], lambda: sink[:])
            t_vones = p.do("vector", lambda e: e.memset(
                VT[:].rearrange("p (t g c) -> p t g c", g=8, c=65)[:, :, :, 64:65], 1.0), sig="v")

            rp_free = [None]

            def rope(src_psum_fn, ct, deps, dst_gate, kq_bf):
                cosA = bass.AP(cos_sb, ct * 32, [[1024, 128], [0, 8], [1, 32], [0, 2]])
                sinA = bass.AP(sin_sb, ct * 32, [[1024, 128], [0, 8], [1, 32], [0, 2]])
                g8 = lambda ap: ap.rearrange("p (u c) -> p u c", u=8)
                g82 = lambda ap: ap.rearrange("p (u f q) -> p u f q", u=8, q=2)
                p.wait("vector", *(list(deps) + [t_cos, t_sin, rp_free[0]]))
                p.do("vector", lambda e: e.tensor_scalar(
                    g8(rp_sh[:])[:, :, 0:64:2], g8(src_psum_fn())[:, :, 1:64:2],
                    -1.0, None, ALU.mult))
                t_c2 = p.do("vector", lambda e: e.tensor_copy(
                    g8(rp_sh[:])[:, :, 1:64:2], g8(src_psum_fn())[:, :, 0:64:2]), sig="v")
                t_t1 = p.do("vector", lambda e: e.tensor_tensor(
                    g82(rp_t1[:]), g82(src_psum_fn()), cosA, ALU.mult), sig="v")
                p.wait("vector", t_c2)
                t_sm = p.do("vector", lambda e: e.tensor_tensor(
                    g82(rp_sh[:]), g82(rp_sh[:]), sinA, ALU.mult), sig="v")
                p.wait("vector", dst_gate, t_t1, t_sm)
                t_add = p.do("vector", lambda e: e.tensor_add(kq_bf[:], rp_t1[:], rp_sh[:]), sig="v")
                rp_free[0] = t_add
                return t_add

            sA1 = g.enter_context(ExitStack())
            wk_sb = sA1.enter_context(nc.sbuf_tensor("wk_sb", [128, 16 * 512], BF16))
            wv_sb = sA1.enter_context(nc.sbuf_tensor("wv_sb", [128, 16 * 512], BF16))
            xnT_kv = sA1.enter_context(nc.sbuf_tensor("xnT_kv", [128, 16 * 128], BF16))
            xn_bfs = [sA1.enter_context(nc.sbuf_tensor(f"xn_bf{i}", [128, 2048], BF16))
                      for i in range(2)]
            xbufs.append(sA1.enter_context(nc.sbuf_tensor("xbuf2", [128, 2048], F32)))
            pA = g.enter_context(ExitStack())
            tpP0 = pA.enter_context(nc.psum_tensor("tpP0", [128, 1024], F32))
            tpP1 = pA.enter_context(nc.psum_tensor("tpP1", [128, 1024], F32))
            tpPs = [tpP0, tpP1]
            kP = pA.enter_context(nc.psum_tensor("kP", [128, 1024], F32))
            vP = pA.enter_context(nc.psum_tensor("vP", [128, 1024], F32))

            t_wk = load(lambda: wk_sb[:].rearrange("p (i c) -> p i c", c=512),
                        lambda: wk.ap().rearrange("(i p) c -> p i c", p=128),
                        eng="gpsimd", sig="cw")
            t_wv = load(lambda: wv_sb[:].rearrange("p (i c) -> p i c", c=512),
                        lambda: wv.ap().rearrange("(i p) c -> p i c", p=128),
                        eng="gpsimd", sig="cw")

            xb_free = [None, None, None]
            stats_free = [None, None, None, None]
            tp_free = [None, None]
            k_free = [None, None]
            v_free = [None, None]
            xn_free = [None, None]
            kq_free = [None, None]
            rstds = [None] * N_A1
            xns = [None] * N_A1
            kvs = [None] * N_A1
            krs = [None] * N_A1
            evss = [None] * N_A1

            t_xs = [None] * N_A1

            def a1_load(t):
                if t >= N_A1:
                    return
                xi = t % 3
                xq = "sync" if t % 2 == 0 else "scalar"
                p.wait(xq, xb_free[xi])
                t_xs[t] = load(lambda xi=xi: xbufs[xi][:],
                               lambda t=t: x[t * 128:(t + 1) * 128, :],
                               eng=xq, sig="lx%d" % xi)

            def a1_front(t):
                xi = t % 3
                xb = xbufs[xi]
                sl = (t % 4) * 8
                rstds[t] = layernorm_stats(lambda xb=xb: xb[:], sl, (t_xs[t],),
                                           stats_free[t % 4])

            def a1_xn(t):
                xi = t % 3
                sl = (t % 4) * 8
                xnb = xn_bfs[t % 2]
                t_var, t_rstd = rstds[t]
                p.wait("vector", t_rstd, t_var, xn_free[t % 2])
                xns[t] = p.do("vector", lambda e, xi=xi, sl=sl, xnb=xnb: e.tensor_scalar(
                    xnb[:], xbufs[xi][:], stats[:, sl + 2:sl + 3], stats[:, sl + 6:sl + 7],
                    ALU.subtract, ALU.mult), sig="v")
                xb_free[xi] = xns[t]
                stats_free[t % 4] = xns[t]

            def a1_ktp(t):
                par = t % 2
                kqb = kq_bfs[par]
                tp = tpPs[par]
                p.wait("tensor", krs[t], tp_free[par])
                t_ktp = None
                for j in range(4):
                    t_ktp = p.do("tensor", lambda e, j=j, tp=tp, kqb=kqb: e.matmul(
                        tp[:, j * 128:(j + 1) * 128], kqb[:, j * 128:(j + 1) * 128],
                        id_bf[:], start=True, stop=True), sig="t" if j == 3 else None)
                kq_free[par] = t_ktp
                p.wait("vector", t_ktp)
                t_kev = p.do("vector", lambda e, t=t, tp=tp: e.tensor_copy(
                    bass.AP(KT, t * 128, [[4 * 4096, 128], [4096, 4], [1, 128]]),
                    tp[:, 0:512].rearrange("p (j c) -> p j c", c=128)), sig="v")
                tp_free[par] = t_kev

            def a1_mid(t):
                par = t % 2
                xnb = xn_bfs[par]
                own = t < 8
                if t >= 1:
                    a1_ktp(t - 1)
                t_tp_last = None
                evs = []
                for hb in range(2):
                    tp = tpPs[par]
                    p.wait("tensor", xns[t], tp_free[par], t_idb)
                    t_tp = None
                    for j in range(8):
                        ic = hb * 8 + j
                        t_tp = p.do("tensor", lambda e, ic=ic, j=j, tp=tp, xnb=xnb: e.matmul(
                            tp[:, j * 128:(j + 1) * 128], xnb[:, ic * 128:(ic + 1) * 128],
                            id_bf[:], start=True, stop=True), sig="t" if j == 7 else None)
                    t_tp_last = t_tp
                    p.wait("vector", t_tp)
                    if own:
                        def ev(e, hb=hb, t=t, tp=tp):
                            dst = bass.AP(xnT_own, (hb * 8) * 1024 + t * 128,
                                          [[16 * 1024, 128], [1024, 8], [1, 128]])
                            return e.tensor_copy(dst, tp[:].rearrange("p (i c) -> p i c", c=128))
                    else:
                        def ev(e, hb=hb, tp=tp):
                            dst = bass.AP(xnT_kv, (hb * 8) * 128,
                                          [[16 * 128, 128], [128, 8], [1, 128]])
                            return e.tensor_copy(dst, tp[:].rearrange("p (i c) -> p i c", c=128))
                    t_ev = p.do("vector", ev, sig="v")
                    tp_free[par] = t_ev
                    evs.append(t_ev)
                xn_free[par] = t_tp_last
                xnT = xnT_own if own else xnT_kv
                colw = 1024 if own else 128
                colo = t * 128 if own else 0
                p.wait("tensor", t_wk, t_wv, k_free[par], v_free[par])
                t_kv = None
                for half in range(2):
                    p.wait("tensor", evs[half])
                    for icl in range(8):
                        ic = half * 8 + icl
                        p.do("tensor", lambda e, ic=ic, par=par, xnT=xnT, colw=colw, colo=colo: e.matmul(
                            kP[:, par * 512:(par + 1) * 512],
                            xnT[:, ic * colw + colo: ic * colw + colo + 128],
                            wk_sb[:, ic * 512:(ic + 1) * 512],
                            start=(ic == 0), stop=(ic == 15)))
                        t_kv = p.do("tensor", lambda e, ic=ic, par=par, xnT=xnT, colw=colw, colo=colo: e.matmul(
                            vP[:, par * 512:(par + 1) * 512],
                            xnT[:, ic * colw + colo: ic * colw + colo + 128],
                            wv_sb[:, ic * 512:(ic + 1) * 512],
                            start=(ic == 0), stop=(ic == 15)), sig="t" if ic == 15 else None)
                kvs[t] = t_kv
                p.wait("vector", t_kv, t_vones)
                t_vev = p.do("vector", lambda e, t=t, par=par: e.tensor_copy(
                    bass.AP(VT, t * 520, [[32 * 520, 128], [65, 8], [1, 64]]),
                    vP[:, par * 512:(par + 1) * 512].rearrange("p (g c) -> p g c", g=8)), sig="v")
                v_free[par] = t_vev
                kqb = kq_bfs[par]
                krs[t] = rope(lambda par=par: kP[:, par * 512:(par + 1) * 512], t, (t_kv,),
                              kq_free[par], kqb)
                k_free[par] = krs[t]

            a1_load(0)
            a1_load(1)
            a1_load(2)
            for i in range(N_A1 + 1):
                if i < N_A1:
                    a1_front(i)
                if i - 1 >= 0:
                    a1_mid(i - 1)
                if i < N_A1:
                    a1_xn(i)
                    a1_load(i + 3)
            a1_ktp(N_A1 - 1)

            dbg_dump("KT", lambda: KT[:], p.barrier_tok("v"))
            dbg_dump("VT", lambda: VT[:], p.barrier_tok("v"))
            dbg_dump("xnT_own", lambda: xnT_own[:], p.barrier_tok("v"))
            p.flush()
            sA1.close()
            full_barrier()
            if BUILD_TO == "a1":
                finish(lambda: xbufs[0][:])
                return nc, dbg_outs

            QT = sKV.enter_context(nc.sbuf_tensor("QT", [128, 16 * 1024], BF16))
            wq_sbs = [sRope.enter_context(nc.sbuf_tensor(f"wq_sb{i}", [128, 16 * 512], BF16))
                      for i in range(2)]
            wq_free = [None, None]
            t_wqs = [None] * 4
            q_free = [k_free[0], k_free[1]]
            t_qs = [None] * 32
            t_qrs = [None] * 32

            def a2_wq_load(q):
                if q >= 4:
                    return
                p.wait("gpsimd", wq_free[q % 2])
                t_wqs[q] = load(
                    lambda q=q: wq_sbs[q % 2][:].rearrange("p (i c) -> p i c", c=512),
                    lambda q=q: wq.ap().rearrange("(i p) c -> p i c", p=128)[:, :, q * 512:(q + 1) * 512],
                    eng="gpsimd", sig="cq")

            def a2_mm(i):
                qtr, tl = i // 8, i % 8
                tpar = tl % 2
                p.wait("tensor", t_wqs[qtr], q_free[tpar])
                t_q = None
                for ic in range(16):
                    t_q = p.do("tensor", lambda e, ic=ic, tl=tl, tpar=tpar, qtr=qtr: e.matmul(
                        kP[:, tpar * 512:(tpar + 1) * 512],
                        xnT_own[:, ic * 1024 + tl * 128: ic * 1024 + tl * 128 + 128],
                        wq_sbs[qtr % 2][:, ic * 512:(ic + 1) * 512],
                        start=(ic == 0), stop=(ic == 15)), sig="t" if ic == 15 else None)
                t_qs[i] = t_q
                if tl == 7:
                    wq_free[qtr % 2] = t_q

            def a2_rope(i):
                tl = i % 8
                tpar = tl % 2
                kqb = kq_bfs[tpar]
                t_qr = rope(lambda tpar=tpar: kP[:, tpar * 512:(tpar + 1) * 512],
                            tl, (t_qs[i],), kq_free[tpar], kqb)
                t_qrs[i] = t_qr
                q_free[tpar] = t_qr

            def a2_qtp(i):
                qtr, tl = i // 8, i % 8
                tpar = tl % 2
                kqb = kq_bfs[tpar]
                tp = tpPs[tpar]
                p.wait("tensor", t_qrs[i], tp_free[tpar])
                t_qtp = None
                for m in range(4):
                    t_qtp = p.do("tensor", lambda e, m=m, tp=tp, kqb=kqb: e.matmul(
                        tp[:, m * 128:(m + 1) * 128], kqb[:, m * 128:(m + 1) * 128],
                        id_bf[:], start=True, stop=True), sig="t" if m == 3 else None)
                kq_free[tpar] = t_qtp
                p.wait("vector", t_qtp)
                t_qev = None
                for w in range(8):
                    m, sr = w // 2, 64 * (w % 2)
                    pi, dr = 4 * qtr + (w % 4), 64 * (w // 4)
                    t_qev = p.do("vector", lambda e, m=m, sr=sr, pi=pi, dr=dr, tl=tl, tp=tp:
                                 e.tensor_copy(
                                     QT[dr:dr + 64, pi * 1024 + tl * 128: pi * 1024 + tl * 128 + 128],
                                     tp[sr:sr + 64, m * 128:(m + 1) * 128]), sig="v" if w == 7 else None)
                tp_free[tpar] = t_qev

            a2_wq_load(0)
            a2_wq_load(1)
            for i in range(34):
                if i < 32:
                    if i % 8 == 0 and i >= 8:
                        a2_wq_load(i // 8 + 1)
                    a2_mm(i)
                if 0 <= i - 1 < 32:
                    a2_rope(i - 1)
                if 0 <= i - 2 < 32:
                    a2_qtp(i - 2)

            dbg_dump("QT", lambda: QT[:], p.barrier_tok("v"))
            p.flush()
            sRope.close()
            pA.close()
            full_barrier()
            if BUILD_TO == "a2":
                finish(lambda: xbufs[0][:])
                return nc, dbg_outs

            sAo = g.enter_context(ExitStack())
            aoT = sAo.enter_context(nc.sbuf_tensor("aoT", [128, 16 * 1024], BF16, side="right"))
            s3 = g.enter_context(ExitStack())
            f0b = [s3.enter_context(nc.sbuf_tensor(f"f0b{i}", [128, 4096], BF16)) for i in range(2)]
            f0_free = [None, None]
            for i in range(64):
                par = i % 2
                p.wait("gpsimd", f0_free[par])
                t_l = load(lambda par=par: f0b[par][:, 0:2048],
                           lambda i=i: upw[i * 128:(i + 1) * 128, :],
                           eng="gpsimd", sig="f0l%d" % par)
                p.wait("sync", t_l)
                f0_free[par] = load(lambda i=i: scr_up[i * 128:(i + 1) * 128, :],
                                    lambda par=par: f0b[par][:, 0:2048], sig="f0s%d" % par)
            for i in range(16):
                for hh in range(2):
                    par = (i * 2 + hh) % 2
                    p.wait("gpsimd", f0_free[par])
                    t_l = load(lambda par=par: f0b[par][:],
                               lambda i=i, hh=hh: dnw[i * 128:(i + 1) * 128,
                                                      hh * 4096:(hh + 1) * 4096],
                               eng="gpsimd", sig="f0l%d" % par)
                    p.wait("sync", t_l)
                    f0_free[par] = load(lambda i=i, hh=hh: scr_dn[i * 128:(i + 1) * 128,
                                                                 hh * 4096:(hh + 1) * 4096],
                                        lambda par=par: f0b[par][:], sig="f0s%d" % par)
            t_f0_done = [p.barrier_tok("f0s0"), p.barrier_tok("f0s1")]
            t_f0_done = [(s_, v_, "_none_", 16) for (s_, v_, _, _) in t_f0_done]
            expA = s3.enter_context(nc.sbuf_tensor("expA", [128, 2 * 1024], BF16))
            expB = s3.enter_context(nc.sbuf_tensor("expB", [128, 2 * 1024], BF16))
            r_sb = [s3.enter_context(nc.sbuf_tensor(f"r_sb{i}", [1, 1024], F32)) for i in range(2)]
            rbc_sb = [s3.enter_context(nc.sbuf_tensor(f"rbc_sb{i}", [64, 1024], F32)) for i in range(2)]
            pvs_sb = [s3.enter_context(nc.sbuf_tensor(f"pvs_sb{i}", [65, 1024], F32)) for i in range(2)]
            p3 = g.enter_context(ExitStack())
            scA = p3.enter_context(nc.psum_tensor("scA", [128, 1024], F32))
            scB = p3.enter_context(nc.psum_tensor("scB", [128, 1024], F32))
            pvA = p3.enter_context(nc.psum_tensor("pvA", [128, 1024], F32))
            pvB = p3.enter_context(nc.psum_tensor("pvB", [128, 1024], F32))

            t_ready = p.barrier_tok("v")
            sc = {0: scA, 1: scB}
            pv = {0: pvA, 1: pvB}
            ex = {0: expA, 1: expB}
            sc_free = {0: t_ready, 1: t_ready}
            pv_free = {0: None, 1: None}
            fin_free = {0: None, 1: None}
            pend = [None]

            def flush_fin():
                if pend[0] is None:
                    return
                heads_, t_rs = pend[0]
                pend[0] = None
                for hx in (0, 1):
                    h = heads_[hx]
                    p.wait("tensor", t_rs[hx], t_ones)
                    t_rb = None
                    for s_ in range(2):
                        t_rb = p.do("tensor", lambda e, s_=s_, hx=hx: e.matmul(
                            pv[hx][0:64, s_ * 512:(s_ + 1) * 512], ones_f[:, 0:64],
                            r_sb[hx][:, s_ * 512:(s_ + 1) * 512], start=True, stop=True),
                            sig="t" if s_ == 1 else None)
                    p.wait("vector", t_rb)
                    t_rc = p.do("vector", lambda e, hx=hx: e.tensor_copy(
                        rbc_sb[hx][:], pv[hx][0:64, :]), sig="v")
                    p.wait("vector", t_rc)
                    t_fin = p.do("vector", lambda e, hx=hx, h=h: e.tensor_tensor(
                        aoT[64 * (h % 2):64 * (h % 2) + 64, (h // 2) * 1024:(h // 2 + 1) * 1024],
                        pvs_sb[hx][0:64, :], rbc_sb[hx][:], ALU.mult), sig="v")
                    pv_free[hx] = t_rc
                    fin_free[hx] = t_fin
            exp_free = {0: [None, None], 1: [None, None]}
            r_free = None
            t_fin = None

            pairs = []
            for pp in range(4):
                for j in range(4):
                    pairs.append((8 * pp + j, 8 * pp + 4 + j))

            def emit_pv(pi, heads, kc, hx, t_exp_kc):
                g_ = heads[hx] // 4
                p.wait("tensor", t_exp_kc)
                if kc == 0:
                    p.wait("tensor", pv_free[hx])
                t_pv = None
                for s_ in range(2):
                    t_pv = p.do("tensor", lambda e, kc=kc, s_=s_, hx=hx, g_=g_: e.matmul(
                        pv[hx][0:65, s_ * 512:(s_ + 1) * 512],
                        VT[:, kc * 520 + g_ * 65: kc * 520 + g_ * 65 + 65],
                        ex[hx][:, (kc % 2) * 1024 + s_ * 512:(kc % 2) * 1024 + (s_ + 1) * 512],
                        start=(kc == 0), stop=(kc == 31)),
                        sig="t" if s_ == 1 else None)
                exp_free[hx][kc % 2] = t_pv
                return t_pv

            for pi, (hA, hB) in enumerate(pairs):
                heads = {0: hA, 1: hB}
                t_exp = {0: None, 1: None}
                t_exp_prev = {0: None, 1: None}
                t_pv = {0: None, 1: None}
                for kc in range(32):
                    for hx in (0, 1):
                        h = heads[hx]
                        gidx = h // 4
                        ptile = gidx // 2
                        rbase = 64 * (gidx % 2)
                        p.wait("tensor", sc_free[hx])
                        tq = None
                        for s_ in range(2):
                            tq = p.do("tensor", lambda e, kc=kc, s_=s_, hx=hx, ptile=ptile,
                                      rbase=rbase, pi=pi: e.matmul(
                                sc[hx][:, s_ * 512:(s_ + 1) * 512],
                                KT[rbase:rbase + 64,
                                   ptile * 4096 + kc * 128: ptile * 4096 + kc * 128 + 128],
                                QT[rbase:rbase + 64,
                                   pi * 1024 + s_ * 512: pi * 1024 + (s_ + 1) * 512],
                                start=True, stop=True), sig="t" if s_ == 1 else None)
                        p.wait("scalar", tq, t_eb, exp_free[hx][kc % 2])
                        t_exp[hx] = p.do("scalar", lambda e, hx=hx, kc=kc: e.activation(
                            ex[hx][:, (kc % 2) * 1024:(kc % 2 + 1) * 1024], sc[hx][:],
                            AF.Exp, bias=ebias[:], scale=0.125), sig="s")
                        sc_free[hx] = t_exp[hx]
                    if kc == 0:
                        flush_fin()
                    if kc >= 1:
                        for hx in (0, 1):
                            t_pv[hx] = emit_pv(pi, heads, kc - 1, hx, t_exp_prev[hx])
                    t_exp_prev = dict(t_exp)
                for hx in (0, 1):
                    t_pv[hx] = emit_pv(pi, heads, 31, hx, t_exp_prev[hx])
                t_rs = {}
                for hx in (0, 1):
                    p.wait("vector", t_pv[hx], fin_free[hx])
                    t_pvs = p.do("vector", lambda e, hx=hx: e.tensor_copy(
                        pvs_sb[hx][:], pv[hx][0:65, :]), sig="v")
                    pv_free[hx] = t_pvs
                for hx in (0, 1):
                    p.wait("vector", pv_free[hx])
                    t_rs[hx] = p.do("vector", lambda e, hx=hx: e.reciprocal(
                        r_sb[hx][:], pvs_sb[hx][64:65, 0:1024]), sig="v")
                pend[0] = (dict(heads), t_rs)

            flush_fin()
            dbg_dump("aoT", lambda: aoT[:], p.barrier_tok("v"))
            p.flush()
            s3.close()
            p3.close()
            sKV.close()
            full_barrier()
            if BUILD_TO == "a3":
                finish(lambda: xbufs[0][:])
                return nc, dbg_outs

            s4 = g.enter_context(ExitStack())
            wo_sb = s4.enter_context(nc.sbuf_tensor("wo_sb", [128, 16 * 2048], BF16, side="right"))
            xr2t = [s4.enter_context(nc.sbuf_tensor(f"xr2t{i}", [128, 2048], F32, side="right")) for i in range(2)]
            xn2_bf = [s4.enter_context(nc.sbuf_tensor(f"xn2_bf{i}", [128, 2048], BF16, side="right"))
                      for i in range(2)]
            aps = s4.enter_context(nc.sbuf_tensor("aps", [128, 2048], F32, side="right"))
            xn2T = g.enter_context(ExitStack())
            xn2T_sb = xn2T.enter_context(nc.sbuf_tensor("xn2T", [128, 16 * 1024], BF16))
            p4 = g.enter_context(ExitStack())
            woP0 = p4.enter_context(nc.psum_tensor("woP0", [128, 2048], F32))
            woP1 = p4.enter_context(nc.psum_tensor("woP1", [128, 2048], F32))
            woPs = [woP0, woP1]

            t_wo = load(lambda: wo_sb[:].rearrange("p (i c) -> p i c", c=2048),
                        lambda: wo.ap().rearrange("(i p) c -> p i c", p=128),
                        eng="gpsimd", sig="co")
            t_aoT_ready = p.barrier_tok("v")
            if A4_STAGE == 1:
                p.wait("sync", t_wo)
                finish(lambda: xbufs[0][:])
                return nc, dbg_outs

            wo_free = [None, None]
            xb_free4 = [xb_free[0], xb_free[1]]
            stats_free4 = [stats_free[0], stats_free[1]]
            xr2_free = [None, None]
            xn2_free = None
            t_spill = [None] * 8
            t_x2ev = None
            xn2s = [None] * 8

            def a4_mm(tl):
                par = tl % 2
                woP = woPs[par]
                xb = xbufs[par]
                p.wait("sync", xb_free4[par])
                t_x = load(lambda xb=xb: xb[:], lambda tl=tl: x[tl * 128:(tl + 1) * 128, :],
                           sig="lx%d" % par)
                p.wait("tensor", t_wo, t_aoT_ready, wo_free[par])
                t_mm = None
                for ic in range(16):
                    for s_ in range(4):
                        t_mm = p.do("tensor", lambda e, ic=ic, s_=s_, tl=tl, woP=woP: e.matmul(
                            woP[:, s_ * 512:(s_ + 1) * 512],
                            aoT[:, ic * 1024 + tl * 128: ic * 1024 + tl * 128 + 128],
                            wo_sb[:, ic * 2048 + s_ * 512: ic * 2048 + (s_ + 1) * 512],
                            start=(ic == 0), stop=(ic == 15)),
                            sig="t" if (ic == 15 and s_ == 3) else None)
                return t_mm, t_x

            def a4_ln(tl, t_mm, t_x):
                par = tl % 2
                woP = woPs[par]
                xb = xbufs[par]
                xrt = xr2t[par]
                sl = par * 8
                t_var, t_rstd = layernorm_stats(lambda woP=woP: woP[:], sl, (t_mm,),
                                                stats_free4[par], psum_in=True)
                p.wait("vector", t_rstd, t_var)
                t_ap = p.do("vector", lambda e, woP=woP, sl=sl: e.tensor_scalar(
                    aps[:], woP[:], stats[:, sl + 2:sl + 3], stats[:, sl + 6:sl + 7],
                    ALU.subtract, ALU.mult), sig="v")
                p.wait("vector", t_x, t_ap, xr2_free[par],
                       t_spill[tl - 2] if tl >= 2 else None)
                t_xr2 = p.do("vector", lambda e, xrt=xrt, xb=xb: e.tensor_add(
                    xrt[:], aps[:], xb[:]), sig="v")
                xb_free4[par] = t_xr2
                p.wait("sync", t_xr2)
                t_spill[tl] = load(lambda tl=tl: xr2sp[tl * 128:(tl + 1) * 128, :],
                                   lambda xrt=xrt: xrt[:], sig="sp%d" % par)
                t_var2, t_rstd2 = layernorm_stats(lambda xrt=xrt: xrt[:], sl, (t_xr2,), t_ap)
                xnb = xn2_bf[par]
                p.wait("vector", t_rstd2, t_var2, xn2_free[par])
                t_xn2 = p.do("vector", lambda e, xrt=xrt, sl=sl, xnb=xnb: e.tensor_scalar(
                    xnb[:], xrt[:], stats[:, sl + 2:sl + 3], stats[:, sl + 6:sl + 7],
                    ALU.subtract, ALU.mult), sig="v")
                stats_free4[par] = t_xn2
                xr2_free[par] = t_xn2
                xn2s[tl] = (t_xn2, t_ap)

            def a4_tp(tl):
                par = tl % 2
                woP = woPs[par]
                xnb = xn2_bf[par]
                t_xn2, t_ap = xn2s[tl]
                t_tp_last = None
                tp_gate = t_ap
                t_ev = None
                for hb in range(2):
                    p.wait("tensor", t_xn2, tp_gate)
                    t_tp = None
                    for j in range(8):
                        ic = hb * 8 + j
                        t_tp = p.do("tensor", lambda e, ic=ic, j=j, woP=woP, xnb=xnb: e.matmul(
                            woP[:, j * 128:(j + 1) * 128], xnb[:, ic * 128:(ic + 1) * 128],
                            id_bf[:], start=True, stop=True), sig="t" if j == 7 else None)
                    t_tp_last = t_tp
                    p.wait("vector", t_tp)
                    t_ev = p.do("vector", lambda e, hb=hb, tl=tl, woP=woP: e.tensor_copy(
                        bass.AP(xn2T_sb, (hb * 8) * 1024 + tl * 128,
                                [[16 * 1024, 128], [1024, 8], [1, 128]]),
                        woP[:, 0:1024].rearrange("p (i c) -> p i c", c=128)), sig="v")
                    tp_gate = t_ev
                xn2_free[par] = t_tp_last
                wo_free[par] = t_ev
                return t_ev

            xn2_free = [None, None]
            mmres = [None] * 8
            for i in range(9):
                if i < 8:
                    mmres[i] = a4_mm(i)
                if i - 1 >= 0:
                    t_x2ev = a4_tp(i - 1)
                if i < 8:
                    a4_ln(i, *mmres[i])

            dbg_dump("xr2", lambda: xr2t[1][:], p.barrier_tok("v"))
            dbg_dump("xn2T", lambda: xn2T_sb[:], p.barrier_tok("v"))
            p.flush()
            s4.close()
            sAo.close()
            p4.close()
            full_barrier()
            if BUILD_TO == "a4":
                finish(lambda: xbufs[0][:])
                return nc, dbg_outs

            dacc = g.enter_context(ExitStack())
            dacc_sb = dacc.enter_context(nc.sbuf_tensor("dacc", [128, 16 * 1024], F32, side="right"))
            sF = g.enter_context(ExitStack())
            hid = sF.enter_context(nc.sbuf_tensor("hid", [128, 16 * 1024], BF16))
            pan = [sF.enter_context(nc.sbuf_tensor(f"pan{i}", [128, 16 * 512], BF16)) for i in range(2)]
            pF = g.enter_context(ExitStack())
            accP = pF.enter_context(nc.psum_tensor("accP", [128, 2048], F32))
            tpF = pF.enter_context(nc.psum_tensor("tpF", [128, 2048], F32))

            t_xn2T_ready = p.barrier_tok("v")
            p.wait("sync", *t_f0_done)
            pan_free = [None, None]
            pan_sg = [None, None]
            acc_free = [None, None]
            hid_free = None
            t_dacc = [None] * 16
            t_dm = None

            for qt in range(4):
                t_hid = [None] * 16
                for pn in range(4):
                    par = pn % 2
                    p.wait("sync", pan_free[par], pan_sg[par])
                    t_pl = None
                    for ic in range(16):
                        t_pl = p.do("sync", lambda e, ic=ic, pn=pn, qt=qt, par=par: e.dma_start(
                            out=pan[par][:, ic * 512:(ic + 1) * 512],
                            in_=scr_up[qt * 2048 + pn * 512: qt * 2048 + (pn + 1) * 512,
                                       ic * 128:(ic + 1) * 128],
                            transpose=True), sig="tu%d" % par, inc=16)
                    p.wait("scalar", t_pl)
                    t_sg = p.do("scalar", lambda e, par=par: e.activation(
                        pan[par][:], pan[par][:], AF.Sign), sig="s")
                    pan_sg[par] = t_sg
                    t_um = None
                    for hcl in range(4):
                        hc = pn * 4 + hcl
                        hq = qt * 16 + hc
                        apar = hc % 2
                        p.wait("tensor", t_sg, acc_free[apar], t_xn2T_ready)
                        for ic in range(16):
                            for s_ in range(2):
                                t_um = p.do("tensor", lambda e, ic=ic, s_=s_, apar=apar,
                                            par=par, hcl=hcl: e.matmul(
                                    accP[:, apar * 1024 + s_ * 512: apar * 1024 + (s_ + 1) * 512],
                                    pan[par][:, ic * 512 + hcl * 128: ic * 512 + hcl * 128 + 128],
                                    xn2T_sb[:, ic * 1024 + s_ * 512: ic * 1024 + (s_ + 1) * 512],
                                    start=(ic == 0), stop=(ic == 15)),
                                    sig="t" if (ic == 15 and s_ == 1) else None)
                        p.wait("vector", t_um, t_upsl, hid_free)
                        t_hid[hc] = p.do("vector", lambda e, hc=hc, apar=apar, hq=hq: e.tensor_scalar(
                            hid[:, hc * 1024:(hc + 1) * 1024],
                            accP[:, apar * 1024:(apar + 1) * 1024],
                            ups_sb[:, hq:hq + 1], 0.0, ALU.mult, ALU.max), sig="v")
                        acc_free[apar] = t_hid[hc]
                    pan_free[par] = t_um
                t_hid_all = t_hid[15]
                for pn in range(4):
                    par = pn % 2
                    p.wait("sync", pan_free[par], pan_sg[par])
                    t_pl = None
                    for j in range(16):
                        t_pl = p.do("sync", lambda e, j=j, pn=pn, qt=qt, par=par: e.dma_start(
                            out=pan[par][:, j * 512:(j + 1) * 512],
                            in_=scr_dn[pn * 512:(pn + 1) * 512,
                                       (qt * 16 + j) * 128:(qt * 16 + j + 1) * 128],
                            transpose=True), sig="tu%d" % par, inc=16)
                    p.wait("scalar", t_pl)
                    t_sg = p.do("scalar", lambda e, par=par: e.activation(
                        pan[par][:], pan[par][:], AF.Sign), sig="s")
                    pan_sg[par] = t_sg
                    t_dm = None
                    for dcl in range(4):
                        dc = pn * 4 + dcl
                        apar = dc % 2
                        p.wait("tensor", t_sg, t_hid_all, acc_free[apar])
                        for j in range(16):
                            for s_ in range(2):
                                t_dm = p.do("tensor", lambda e, j=j, s_=s_, apar=apar,
                                            par=par, dcl=dcl: e.matmul(
                                    accP[:, apar * 1024 + s_ * 512: apar * 1024 + (s_ + 1) * 512],
                                    pan[par][:, j * 512 + dcl * 128: j * 512 + dcl * 128 + 128],
                                    hid[:, j * 1024 + s_ * 512: j * 1024 + (s_ + 1) * 512],
                                    start=(j == 0), stop=(j == 15)),
                                    sig="t" if (j == 15 and s_ == 1) else None)
                        p.wait("vector", t_dm)
                        if qt == 0:
                            t_dacc[dc] = p.do("vector", lambda e, dc=dc, apar=apar: e.tensor_copy(
                                dacc_sb[:, dc * 1024:(dc + 1) * 1024],
                                accP[:, apar * 1024:(apar + 1) * 1024]), sig="v")
                        else:
                            p.wait("vector", t_dacc[dc])
                            t_dacc[dc] = p.do("vector", lambda e, dc=dc, apar=apar: e.tensor_add(
                                dacc_sb[:, dc * 1024:(dc + 1) * 1024],
                                accP[:, apar * 1024:(apar + 1) * 1024],
                                dacc_sb[:, dc * 1024:(dc + 1) * 1024]), sig="v")
                        acc_free[apar] = t_dacc[dc]
                    pan_free[par] = t_dm
                hid_free = t_dm

            dbg_dump("dacc", lambda: dacc_sb[:], p.barrier_tok("v"))
            p.flush()
            sF.close()
            xn2T.close()
            full_barrier()
            if BUILD_TO == "ffn":
                finish(lambda: xbufs[0][:])
                pF.close()
                dacc.close()
                return nc, dbg_outs

            sfin = dacc.enter_context(nc.sbuf_tensor("sfin", [128, 16], F32, side="right"))
            outb = [dacc.enter_context(nc.sbuf_tensor(f"outb{i}", [128, 2048], F32, side="right")) for i in range(2)]
            aps = dacc.enter_context(nc.sbuf_tensor("apsF", [128, 2048], F32, side="right"))
            p.wait("tensor", t_outsl, t_ones, p.barrier_tok("v"))
            t_os = p.do("tensor", lambda e: e.matmul(
                accP[:, 0:1], ones_f[:, 0:128], outs_sb[:], start=True, stop=True), sig="t")
            p.wait("vector", t_os, t_dnsl)
            t_sf = p.do("vector", lambda e: e.tensor_scalar(
                sfin[:], dns_sb[:], accP[:, 0:1], None, ALU.mult), sig="v")
            t_dsc = [None] * 16
            for dc in range(16):
                p.wait("vector", t_sf, t_dacc[dc])
                t_dsc[dc] = p.do("vector", lambda e, dc=dc: e.tensor_scalar(
                    dacc_sb[:, dc * 1024:(dc + 1) * 1024], dacc_sb[:, dc * 1024:(dc + 1) * 1024],
                    sfin[:, dc:dc + 1], None, ALU.mult), sig="v")

            stats_free6 = [stats_free4[0], stats_free4[1]]
            xb_free6 = [xb_free4[0], xb_free4[1]]
            ftp_free = [None, None]
            aps_free = None
            out_done = [None] * 8
            for tl in range(8):
                par = tl % 2
                sl = par * 8
                xb = xbufs[par]
                p.wait("sync", xb_free6[par], t_spill[tl])
                t_x2 = load(lambda xb=xb: xb[:], lambda tl=tl: xr2sp[tl * 128:(tl + 1) * 128, :],
                            sig="lx%d" % par)
                t_fev = None
                for hb in range(4):
                    p.wait("tensor", ftp_free[par], t_idf, t_os)
                    t_tp = None
                    for j in range(4):
                        dc = hb * 4 + j
                        p.wait("tensor", t_dsc[dc])
                        t_tp = p.do("tensor", lambda e, dc=dc, j=j, par=par, tl=tl: e.transpose(
                            tpF[:, par * 1024 + j * 128: par * 1024 + j * 128 + 128],
                            dacc_sb[:, dc * 1024 + tl * 128: dc * 1024 + tl * 128 + 128],
                            id_f[:]), sig="t" if j == 3 else None)
                    p.wait("vector", t_tp, aps_free)
                    t_fev = p.do("vector", lambda e, hb=hb, par=par: e.tensor_copy(
                        aps[:, hb * 512:(hb + 1) * 512],
                        tpF[:, par * 1024: par * 1024 + 512]), sig="v")
                    ftp_free[par] = t_fev
                t_var, t_rstd = layernorm_stats(lambda: aps[:], sl, (t_fev,), stats_free6[par])
                p.wait("vector", t_rstd, t_var, t_x2, out_done[tl - 2] if tl >= 2 else None)
                t_o = p.do("vector", lambda e, par=par, sl=sl: e.tensor_scalar(
                    outb[par][:], aps[:], stats[:, sl + 2:sl + 3], stats[:, sl + 6:sl + 7],
                    ALU.subtract, ALU.mult), sig="v")
                aps_free = t_o
                p.wait("vector", t_o)
                t_o2 = p.do("vector", lambda e, par=par, xb=xb: e.tensor_add(
                    outb[par][:], outb[par][:], xb[:]), sig="v")
                stats_free6[par] = t_o2
                xb_free6[par] = t_o2
                p.wait("sync", t_o2)
                out_done[tl] = p.do("sync", lambda e, tl=tl, par=par: e.dma_start(
                    out=out[tl * 128:(tl + 1) * 128, :], in_=outb[par][:]), sig="so%d" % par, inc=16)

            all_dma_barrier("sync")
            p.flush()
            pF.close()
            dacc.close()
    return nc, dbg_outs


def _tables(qc):
    pos = (np.arange(4096) + qc * 1024) % 4096
    f = np.arange(32)
    inv = 10000.0 ** (-2.0 * f / 64.0)
    ang = pos[:, None] * inv[None, :]
    cos = np.cos(ang).astype(np.float32)
    sin = np.sin(ang).astype(np.float32)
    cosk = np.zeros((128, 1024), np.float32)
    sink = np.zeros((128, 1024), np.float32)
    for t in range(32):
        cosk[:, t * 32:(t + 1) * 32] = cos[t * 128:(t + 1) * 128]
        sink[:, t * 32:(t + 1) * 32] = sin[t * 128:(t + 1) * 128]
    return cosk, sink


def make_in_maps(L0_attn_x, L0_attn_wq, L0_attn_wk, L0_attn_wv, L0_attn_wo,
                 L0_ffn_up_weight, L0_ffn_down_weight, L0_ffn_up_scale,
                 L0_ffn_down_scale, L0_ffn_output_scale):
    in_maps = []
    for c in range(8):
        b, qc = c // 4, c % 4
        xr = np.roll(np.asarray(L0_attn_x[b]), -qc * 1024, axis=0)
        ck, sk = _tables(qc)
        in_maps.append({
            "x": np.ascontiguousarray(xr),
            "wq": np.asarray(L0_attn_wq), "wk": np.asarray(L0_attn_wk),
            "wv": np.asarray(L0_attn_wv), "wo": np.asarray(L0_attn_wo),
            "upw": np.asarray(L0_ffn_up_weight), "dnw": np.asarray(L0_ffn_down_weight),
            "ups": np.ascontiguousarray(np.asarray(L0_ffn_up_scale).reshape(64, 128).T),
            "dns": np.ascontiguousarray(np.asarray(L0_ffn_down_scale).reshape(16, 128).T),
            "outsc": np.asarray(L0_ffn_output_scale).reshape(1, 1),
            "cosk": ck, "sink": sk,
            "ident": np.eye(128, dtype=np.float32),
            "onesr": np.ones((1, 128), np.float32),
        })
    return in_maps


def kernel(**inputs):
    nc, _ = build_nc()
    in_maps = make_in_maps(**inputs)
    res = run_bass_kernel_spmd(nc, in_maps, core_ids=list(range(8)))
    outp = np.empty((2, 4096, 2048), np.float32)
    for c in range(8):
        b, qc = c // 4, c % 4
        outp[b, qc * 1024:(qc + 1) * 1024] = res.results[c]["out"]
    return outp


# version 14
# speedup vs baseline: 1.0357x; 1.0048x over previous
import numpy as np
from contextlib import ExitStack

import concourse.bass as bass
import concourse.mybir as mybir
from concourse.bass_utils import run_bass_kernel_spmd

F32 = mybir.dt.float32
BF16 = mybir.dt.bfloat16
AF = mybir.ActivationFunctionType
ALU = mybir.AluOpType
AX = mybir.AxisListType

ENGINES = ("sync", "gpsimd", "tensor", "vector", "scalar")

BUILD_TO = "all"
N_A1 = 32
A4_STAGE = 3
DBG = {}


class P:

    def __init__(self, nc, block):
        self.nc = nc
        self.block = block
        self.buf = {e: [] for e in ENGINES}
        self.cnt = {}
        self.sems = {}
        self.hwm = {e: {} for e in ENGINES}

    def sem(self, name, stack):
        h = stack.enter_context(self.nc.semaphore(name))
        self.sems[name] = h
        self.cnt[name] = 0

    SERIAL = False

    def do(self, eng, fn, sig=None, inc=1):
        if P.SERIAL:
            for s in list(self.cnt):
                self.wait(eng, (s, self.cnt[s], "_none_", 2))
        if sig is None:
            self.buf[eng].append(("op", fn, None, 0))
            return None
        self.cnt[sig] += inc
        val = self.cnt[sig]
        self.buf[eng].append(("op", fn, sig, inc))
        return (sig, val, eng, inc)

    def wait(self, eng, *toks):
        for tok in toks:
            if tok is None:
                continue
            sig, val, src, inc = tok
            if inc == 16:
                val = self.cnt[sig]
            if self.hwm[eng].get(sig, 0) >= val:
                continue
            self.hwm[eng][sig] = val
            self.buf[eng].append(("wait", sig, val))

    def barrier_tok(self, sig):
        return (sig, self.cnt[sig], "_none_", 2)

    def flush(self):
        for eng in ENGINES:
            items = self.buf[eng]
            if not items:
                continue
            self.buf[eng] = []
            dec = getattr(self.block, eng)
            sems = self.sems

            def body(e, items=items):
                for it in items:
                    if it[0] == "op":
                        _, fn, sig, inc = it
                        ins = fn(e)
                        if sig is not None:
                            ins.then_inc(sems[sig], inc)
                    else:
                        _, sig, val = it
                        e.wait_ge(sems[sig], val)
            dec(body)


def build_nc():
    nc = bass.Bass("TRN2", target_bir_lowering=False)

    x = nc.dram_tensor("x", [4096, 2048], F32, kind="ExternalInput")
    wq = nc.dram_tensor("wq", [2048, 2048], F32, kind="ExternalInput")
    wk = nc.dram_tensor("wk", [2048, 512], F32, kind="ExternalInput")
    wv = nc.dram_tensor("wv", [2048, 512], F32, kind="ExternalInput")
    wo = nc.dram_tensor("wo", [2048, 2048], F32, kind="ExternalInput")
    upw = nc.dram_tensor("upw", [8192, 2048], F32, kind="ExternalInput")
    dnw = nc.dram_tensor("dnw", [2048, 8192], F32, kind="ExternalInput")
    ups = nc.dram_tensor("ups", [128, 64], F32, kind="ExternalInput")
    dns = nc.dram_tensor("dns", [128, 16], F32, kind="ExternalInput")
    outsc = nc.dram_tensor("outsc", [1, 1], F32, kind="ExternalInput")
    cosk = nc.dram_tensor("cosk", [128, 1024], F32, kind="ExternalInput")
    sink = nc.dram_tensor("sink", [128, 1024], F32, kind="ExternalInput")
    ident = nc.dram_tensor("ident", [128, 128], F32, kind="ExternalInput")
    onesr = nc.dram_tensor("onesr", [1, 128], F32, kind="ExternalInput")
    out = nc.dram_tensor("out", [1024, 2048], F32, kind="ExternalOutput")
    xr2sp = nc.dram_tensor("xr2sp", [1024, 2048], F32, kind="Internal")
    scr_up = nc.dram_tensor("scr_up", [8192, 2048], BF16, kind="Internal")
    r_dram = [nc.dram_tensor(f"r_dram{i}", [1, 1024], F32, kind="Internal") for i in range(2)]
    scr_dn = nc.dram_tensor("scr_dn", [2048, 8192], BF16, kind="Internal")

    dbg_outs = {}
    for name, (shape, dt) in DBG.items():
        dbg_outs[name] = nc.dram_tensor("dbg_" + name, list(shape), dt, kind="ExternalOutput")

    g = ExitStack()
    with g:
        with nc.Block() as block:
            p = P(nc, block)
            DMA_SEMS = ("lc", "lx0", "lx1", "cw", "cq", "co", "cu0", "cu1",
                        "cd0", "cd1", "sp0", "sp1", "so0", "so1", "dbg", "lx2",
                        "f0l0", "f0l1", "f0s0", "f0s1", "tu0", "tu1", "rb0", "rb1")
            for s in DMA_SEMS + ("t", "v", "s"):
                p.sem(s, g)

            def all_dma_barrier(eng):
                for s in DMA_SEMS:
                    if p.cnt[s]:
                        p.wait(eng, (s, p.cnt[s], "_none_", 16))

            def full_barrier():
                for eng in ENGINES:
                    for s_ in ("t", "v", "s"):
                        if p.cnt[s_]:
                            p.wait(eng, (s_, p.cnt[s_], "_none_", 2))
                    all_dma_barrier(eng)

            def dbg_dump(name, ap_fn, *deps):
                if name not in DBG:
                    return
                p.wait("sync", *deps)
                p.do("sync", lambda e, f=ap_fn, d=dbg_outs[name]: e.dma_start(out=d[:], in_=f()),
                     sig="dbg", inc=16)

            def load(dst_ap_fn, src_ap_fn, *, eng="sync", sig="lc"):
                return p.do(eng, lambda e: e.dma_start(out=dst_ap_fn(), in_=src_ap_fn()),
                            sig=sig, inc=16)

            def finish(src_fn):
                for tl in range(8):
                    p.do("sync", lambda e, tl=tl: e.dma_start(
                        out=out[tl * 128:(tl + 1) * 128, :], in_=src_fn()), sig="so0", inc=16)
                all_dma_barrier("sync")
                p.flush()

            id_bf = g.enter_context(nc.sbuf_tensor("id_bf", [128, 128], BF16))
            id_f = g.enter_context(nc.sbuf_tensor("id_f", [128, 128], F32))
            ones_f = g.enter_context(nc.sbuf_tensor("ones_f", [1, 128], F32))
            ups_sb = g.enter_context(nc.sbuf_tensor("ups_sb", [128, 64], F32))
            dns_sb = g.enter_context(nc.sbuf_tensor("dns_sb", [128, 16], F32))
            outs_sb = g.enter_context(nc.sbuf_tensor("outs_sb", [1, 1], F32))
            ebias = g.enter_context(nc.sbuf_tensor("ebias", [128, 1], F32))
            epsb = g.enter_context(nc.sbuf_tensor("epsb", [128, 1], F32))
            stats = g.enter_context(nc.sbuf_tensor("stats", [128, 32], F32))
            sqscr = g.enter_context(nc.sbuf_tensor("sqscr", [128, 2048], F32))
            xbufs = [g.enter_context(nc.sbuf_tensor(f"xbuf{i}", [128, 2048], F32))
                     for i in range(2)]

            t_idb = load(lambda: id_bf[:], lambda: ident[:], eng="gpsimd", sig="cw")
            t_idf = load(lambda: id_f[:], lambda: ident[:])
            t_ones = load(lambda: ones_f[:], lambda: onesr[:])
            t_upsl = load(lambda: ups_sb[:], lambda: ups[:])
            t_dnsl = load(lambda: dns_sb[:], lambda: dns[:])
            t_outsl = load(lambda: outs_sb[:], lambda: outsc[:])
            t_eb = p.do("vector", lambda e: e.memset(ebias[:], -20.0), sig="v")
            t_eps = p.do("vector", lambda e: e.memset(epsb[:], 1e-5), sig="v")

            def layernorm_stats(ap_fn, sl, deps, stats_gate, psum_in=False):
                p.wait("scalar", *(list(deps) + [stats_gate]))
                t_ssq = p.do("scalar", lambda e: e.activation(
                    sqscr[:], ap_fn(), AF.Square, accum_out=stats[:, sl + 1:sl + 2]), sig="s")
                t_sum = p.do("scalar", lambda e: e.activation(
                    sqscr[:], ap_fn(), AF.Copy, accum_out=stats[:, sl:sl + 1]), sig="s")
                p.wait("vector", *(list(deps) + [stats_gate, t_sum]))
                t_mean = p.do("vector", lambda e: e.tensor_scalar(
                    stats[:, sl + 2:sl + 3], stats[:, sl:sl + 1], 1.0 / 2048, None, ALU.mult), sig="v")
                p.wait("vector", t_mean)
                t_msq = p.do("vector", lambda e: e.tensor_tensor(
                    stats[:, sl + 3:sl + 4], stats[:, sl + 2:sl + 3], stats[:, sl + 2:sl + 3], ALU.mult), sig="v")
                p.wait("vector", t_msq, t_ssq)
                t_var = p.do("vector", lambda e: e.scalar_tensor_tensor(
                    stats[:, sl + 4:sl + 5], stats[:, sl + 1:sl + 2], 1.0 / 2048,
                    stats[:, sl + 3:sl + 4], ALU.mult, ALU.subtract), sig="v")
                p.wait("scalar", t_var, t_eps)
                t_ln = p.do("scalar", lambda e: e.activation(
                    stats[:, sl + 5:sl + 6], stats[:, sl + 4:sl + 5], AF.Ln, bias=epsb[:]), sig="s")
                p.wait("scalar", t_ln)
                t_rstd = p.do("scalar", lambda e: e.activation(
                    stats[:, sl + 6:sl + 7], stats[:, sl + 5:sl + 6], AF.Exp, scale=-0.5), sig="s")
                return t_var, t_rstd

            sKV = g.enter_context(ExitStack())
            KT = sKV.enter_context(nc.sbuf_tensor("KT", [128, 4 * 4096], BF16))
            VT = sKV.enter_context(nc.sbuf_tensor("VT", [128, 32 * 520], BF16))

            sRope = g.enter_context(ExitStack())
            cos_sb = sRope.enter_context(nc.sbuf_tensor("cos_sb", [128, 1024], F32, side="right"))
            sin_sb = sRope.enter_context(nc.sbuf_tensor("sin_sb", [128, 1024], F32, side="right"))
            kq_bfs = [sRope.enter_context(nc.sbuf_tensor(f"kq_bf{i}", [128, 512], BF16, side="right"))
                      for i in range(2)]
            rp_sh = sRope.enter_context(nc.sbuf_tensor("rp_sh", [128, 512], F32, side="right"))
            rp_t1 = sRope.enter_context(nc.sbuf_tensor("rp_t1", [128, 512], F32, side="right"))
            xnT_own = sRope.enter_context(nc.sbuf_tensor("xnT_own", [128, 16 * 1024], BF16, side="right"))

            t_cos = load(lambda: cos_sb[:], lambda: cosk[:])
            t_sin = load(lambda: sin_sb[:], lambda: sink[:])
            t_vones = p.do("vector", lambda e: e.memset(
                VT[:].rearrange("p (t g c) -> p t g c", g=8, c=65)[:, :, :, 64:65], 1.0), sig="v")

            rp_free = [None]

            def rope(src_psum_fn, ct, deps, dst_gate, kq_bf):
                cosA = bass.AP(cos_sb, ct * 32, [[1024, 128], [0, 8], [1, 32], [0, 2]])
                sinA = bass.AP(sin_sb, ct * 32, [[1024, 128], [0, 8], [1, 32], [0, 2]])
                g8 = lambda ap: ap.rearrange("p (u c) -> p u c", u=8)
                g82 = lambda ap: ap.rearrange("p (u f q) -> p u f q", u=8, q=2)
                p.wait("vector", *(list(deps) + [t_cos, t_sin, rp_free[0]]))
                p.do("vector", lambda e: e.tensor_scalar(
                    g8(rp_sh[:])[:, :, 0:64:2], g8(src_psum_fn())[:, :, 1:64:2],
                    -1.0, None, ALU.mult))
                t_c2 = p.do("vector", lambda e: e.tensor_copy(
                    g8(rp_sh[:])[:, :, 1:64:2], g8(src_psum_fn())[:, :, 0:64:2]), sig="v")
                t_t1 = p.do("vector", lambda e: e.tensor_tensor(
                    g82(rp_t1[:]), g82(src_psum_fn()), cosA, ALU.mult), sig="v")
                p.wait("vector", t_c2)
                t_sm = p.do("vector", lambda e: e.tensor_tensor(
                    g82(rp_sh[:]), g82(rp_sh[:]), sinA, ALU.mult), sig="v")
                p.wait("vector", dst_gate, t_t1, t_sm)
                t_add = p.do("vector", lambda e: e.tensor_add(kq_bf[:], rp_t1[:], rp_sh[:]), sig="v")
                rp_free[0] = t_add
                return t_add

            sA1 = g.enter_context(ExitStack())
            wk_sb = sA1.enter_context(nc.sbuf_tensor("wk_sb", [128, 16 * 512], BF16))
            wv_sb = sA1.enter_context(nc.sbuf_tensor("wv_sb", [128, 16 * 512], BF16))
            xnT_kv = sA1.enter_context(nc.sbuf_tensor("xnT_kv", [128, 16 * 128], BF16))
            xn_bfs = [sA1.enter_context(nc.sbuf_tensor(f"xn_bf{i}", [128, 2048], BF16))
                      for i in range(2)]
            xbufs.append(sA1.enter_context(nc.sbuf_tensor("xbuf2", [128, 2048], F32)))
            pA = g.enter_context(ExitStack())
            tpP0 = pA.enter_context(nc.psum_tensor("tpP0", [128, 1024], F32))
            tpP1 = pA.enter_context(nc.psum_tensor("tpP1", [128, 1024], F32))
            tpPs = [tpP0, tpP1]
            kP = pA.enter_context(nc.psum_tensor("kP", [128, 1024], F32))
            vP = pA.enter_context(nc.psum_tensor("vP", [128, 1024], F32))

            t_wk = load(lambda: wk_sb[:].rearrange("p (i c) -> p i c", c=512),
                        lambda: wk.ap().rearrange("(i p) c -> p i c", p=128),
                        eng="gpsimd", sig="cw")
            t_wv = load(lambda: wv_sb[:].rearrange("p (i c) -> p i c", c=512),
                        lambda: wv.ap().rearrange("(i p) c -> p i c", p=128),
                        eng="gpsimd", sig="cw")

            xb_free = [None, None, None]
            stats_free = [None, None, None, None]
            tp_free = [None, None]
            k_free = [None, None]
            v_free = [None, None]
            xn_free = [None, None]
            kq_free = [None, None]
            rstds = [None] * N_A1
            xns = [None] * N_A1
            kvs = [None] * N_A1
            krs = [None] * N_A1
            evss = [None] * N_A1

            t_xs = [None] * N_A1

            def a1_load(t):
                if t >= N_A1:
                    return
                xi = t % 3
                xq = "sync" if t % 2 == 0 else "scalar"
                p.wait(xq, xb_free[xi])
                t_xs[t] = load(lambda xi=xi: xbufs[xi][:],
                               lambda t=t: x[t * 128:(t + 1) * 128, :],
                               eng=xq, sig="lx%d" % xi)

            def a1_front(t):
                xi = t % 3
                xb = xbufs[xi]
                sl = (t % 4) * 8
                rstds[t] = layernorm_stats(lambda xb=xb: xb[:], sl, (t_xs[t],),
                                           stats_free[t % 4])

            def a1_xn(t):
                xi = t % 3
                sl = (t % 4) * 8
                xnb = xn_bfs[t % 2]
                t_var, t_rstd = rstds[t]
                p.wait("vector", t_rstd, t_var, xn_free[t % 2])
                xns[t] = p.do("vector", lambda e, xi=xi, sl=sl, xnb=xnb: e.tensor_scalar(
                    xnb[:], xbufs[xi][:], stats[:, sl + 2:sl + 3], stats[:, sl + 6:sl + 7],
                    ALU.subtract, ALU.mult), sig="v")
                xb_free[xi] = xns[t]
                stats_free[t % 4] = xns[t]

            def a1_ktp(t):
                par = t % 2
                kqb = kq_bfs[par]
                tp = tpPs[par]
                p.wait("tensor", krs[t], tp_free[par])
                t_ktp = None
                for j in range(4):
                    t_ktp = p.do("tensor", lambda e, j=j, tp=tp, kqb=kqb: e.matmul(
                        tp[:, j * 128:(j + 1) * 128], kqb[:, j * 128:(j + 1) * 128],
                        id_bf[:], start=True, stop=True), sig="t" if j == 3 else None)
                kq_free[par] = t_ktp
                p.wait("vector", t_ktp)
                t_kev = p.do("vector", lambda e, t=t, tp=tp: e.tensor_copy(
                    bass.AP(KT, t * 128, [[4 * 4096, 128], [4096, 4], [1, 128]]),
                    tp[:, 0:512].rearrange("p (j c) -> p j c", c=128)), sig="v")
                tp_free[par] = t_kev

            def a1_mid(t):
                par = t % 2
                xnb = xn_bfs[par]
                own = t < 8
                if t >= 1:
                    a1_ktp(t - 1)
                t_tp_last = None
                evs = []
                for hb in range(2):
                    tp = tpPs[par]
                    p.wait("tensor", xns[t], tp_free[par], t_idb)
                    t_tp = None
                    for j in range(8):
                        ic = hb * 8 + j
                        t_tp = p.do("tensor", lambda e, ic=ic, j=j, tp=tp, xnb=xnb: e.matmul(
                            tp[:, j * 128:(j + 1) * 128], xnb[:, ic * 128:(ic + 1) * 128],
                            id_bf[:], start=True, stop=True), sig="t" if j == 7 else None)
                    t_tp_last = t_tp
                    p.wait("vector", t_tp)
                    if own:
                        def ev(e, hb=hb, t=t, tp=tp):
                            dst = bass.AP(xnT_own, (hb * 8) * 1024 + t * 128,
                                          [[16 * 1024, 128], [1024, 8], [1, 128]])
                            return e.tensor_copy(dst, tp[:].rearrange("p (i c) -> p i c", c=128))
                    else:
                        def ev(e, hb=hb, tp=tp):
                            dst = bass.AP(xnT_kv, (hb * 8) * 128,
                                          [[16 * 128, 128], [128, 8], [1, 128]])
                            return e.tensor_copy(dst, tp[:].rearrange("p (i c) -> p i c", c=128))
                    t_ev = p.do("vector", ev, sig="v")
                    tp_free[par] = t_ev
                    evs.append(t_ev)
                xn_free[par] = t_tp_last
                xnT = xnT_own if own else xnT_kv
                colw = 1024 if own else 128
                colo = t * 128 if own else 0
                p.wait("tensor", t_wk, t_wv, k_free[par], v_free[par])
                t_kv = None
                for half in range(2):
                    p.wait("tensor", evs[half])
                    for icl in range(8):
                        ic = half * 8 + icl
                        p.do("tensor", lambda e, ic=ic, par=par, xnT=xnT, colw=colw, colo=colo: e.matmul(
                            kP[:, par * 512:(par + 1) * 512],
                            xnT[:, ic * colw + colo: ic * colw + colo + 128],
                            wk_sb[:, ic * 512:(ic + 1) * 512],
                            start=(ic == 0), stop=(ic == 15)))
                        t_kv = p.do("tensor", lambda e, ic=ic, par=par, xnT=xnT, colw=colw, colo=colo: e.matmul(
                            vP[:, par * 512:(par + 1) * 512],
                            xnT[:, ic * colw + colo: ic * colw + colo + 128],
                            wv_sb[:, ic * 512:(ic + 1) * 512],
                            start=(ic == 0), stop=(ic == 15)), sig="t" if ic == 15 else None)
                kvs[t] = t_kv
                p.wait("vector", t_kv, t_vones)
                t_vev = p.do("vector", lambda e, t=t, par=par: e.tensor_copy(
                    bass.AP(VT, t * 520, [[32 * 520, 128], [65, 8], [1, 64]]),
                    vP[:, par * 512:(par + 1) * 512].rearrange("p (g c) -> p g c", g=8)), sig="v")
                v_free[par] = t_vev
                kqb = kq_bfs[par]
                krs[t] = rope(lambda par=par: kP[:, par * 512:(par + 1) * 512], t, (t_kv,),
                              kq_free[par], kqb)
                k_free[par] = krs[t]

            a1_load(0)
            a1_load(1)
            a1_load(2)
            for i in range(N_A1 + 1):
                if i < N_A1:
                    a1_front(i)
                if i - 1 >= 0:
                    a1_mid(i - 1)
                if i < N_A1:
                    a1_xn(i)
                    a1_load(i + 3)
            a1_ktp(N_A1 - 1)

            dbg_dump("KT", lambda: KT[:], p.barrier_tok("v"))
            dbg_dump("VT", lambda: VT[:], p.barrier_tok("v"))
            dbg_dump("xnT_own", lambda: xnT_own[:], p.barrier_tok("v"))
            p.flush()
            sA1.close()
            full_barrier()
            if BUILD_TO == "a1":
                finish(lambda: xbufs[0][:])
                return nc, dbg_outs

            QT = sKV.enter_context(nc.sbuf_tensor("QT", [128, 16 * 1024], BF16))
            wq_sbs = [sRope.enter_context(nc.sbuf_tensor(f"wq_sb{i}", [128, 16 * 512], BF16))
                      for i in range(2)]
            wq_free = [None, None]
            t_wqs = [None] * 4
            q_free = [k_free[0], k_free[1]]
            t_qs = [None] * 32
            t_qrs = [None] * 32

            def a2_wq_load(q):
                if q >= 4:
                    return
                p.wait("gpsimd", wq_free[q % 2])
                t_wqs[q] = load(
                    lambda q=q: wq_sbs[q % 2][:].rearrange("p (i c) -> p i c", c=512),
                    lambda q=q: wq.ap().rearrange("(i p) c -> p i c", p=128)[:, :, q * 512:(q + 1) * 512],
                    eng="gpsimd", sig="cq")

            def a2_mm(i):
                qtr, tl = i // 8, i % 8
                tpar = tl % 2
                p.wait("tensor", t_wqs[qtr], q_free[tpar])
                t_q = None
                for ic in range(16):
                    t_q = p.do("tensor", lambda e, ic=ic, tl=tl, tpar=tpar, qtr=qtr: e.matmul(
                        kP[:, tpar * 512:(tpar + 1) * 512],
                        xnT_own[:, ic * 1024 + tl * 128: ic * 1024 + tl * 128 + 128],
                        wq_sbs[qtr % 2][:, ic * 512:(ic + 1) * 512],
                        start=(ic == 0), stop=(ic == 15)), sig="t" if ic == 15 else None)
                t_qs[i] = t_q
                if tl == 7:
                    wq_free[qtr % 2] = t_q

            def a2_rope(i):
                tl = i % 8
                tpar = tl % 2
                kqb = kq_bfs[tpar]
                t_qr = rope(lambda tpar=tpar: kP[:, tpar * 512:(tpar + 1) * 512],
                            tl, (t_qs[i],), kq_free[tpar], kqb)
                t_qrs[i] = t_qr
                q_free[tpar] = t_qr

            def a2_qtp(i):
                qtr, tl = i // 8, i % 8
                tpar = tl % 2
                kqb = kq_bfs[tpar]
                tp = tpPs[tpar]
                p.wait("tensor", t_qrs[i], tp_free[tpar])
                t_qtp = None
                for m in range(4):
                    t_qtp = p.do("tensor", lambda e, m=m, tp=tp, kqb=kqb: e.matmul(
                        tp[:, m * 128:(m + 1) * 128], kqb[:, m * 128:(m + 1) * 128],
                        id_bf[:], start=True, stop=True), sig="t" if m == 3 else None)
                kq_free[tpar] = t_qtp
                p.wait("vector", t_qtp)
                t_qev = None
                for w in range(8):
                    m, sr = w // 2, 64 * (w % 2)
                    pi, dr = 4 * qtr + (w % 4), 64 * (w // 4)
                    t_qev = p.do("vector", lambda e, m=m, sr=sr, pi=pi, dr=dr, tl=tl, tp=tp:
                                 e.tensor_copy(
                                     QT[dr:dr + 64, pi * 1024 + tl * 128: pi * 1024 + tl * 128 + 128],
                                     tp[sr:sr + 64, m * 128:(m + 1) * 128]), sig="v" if w == 7 else None)
                tp_free[tpar] = t_qev

            a2_wq_load(0)
            a2_wq_load(1)
            for i in range(34):
                if i < 32:
                    if i % 8 == 0 and i >= 8:
                        a2_wq_load(i // 8 + 1)
                    a2_mm(i)
                if 0 <= i - 1 < 32:
                    a2_rope(i - 1)
                if 0 <= i - 2 < 32:
                    a2_qtp(i - 2)

            dbg_dump("QT", lambda: QT[:], p.barrier_tok("v"))
            p.flush()
            sRope.close()
            pA.close()
            full_barrier()
            if BUILD_TO == "a2":
                finish(lambda: xbufs[0][:])
                return nc, dbg_outs

            sAo = g.enter_context(ExitStack())
            aoT = sAo.enter_context(nc.sbuf_tensor("aoT", [128, 16 * 1024], BF16, side="right"))
            s3 = g.enter_context(ExitStack())
            f0b = [s3.enter_context(nc.sbuf_tensor(f"f0b{i}", [128, 4096], BF16)) for i in range(2)]
            f0_free = [None, None]
            for i in range(64):
                par = i % 2
                p.wait("gpsimd", f0_free[par])
                t_l = load(lambda par=par: f0b[par][:, 0:2048],
                           lambda i=i: upw[i * 128:(i + 1) * 128, :],
                           eng="gpsimd", sig="f0l%d" % par)
                p.wait("sync", t_l)
                f0_free[par] = load(lambda i=i: scr_up[i * 128:(i + 1) * 128, :],
                                    lambda par=par: f0b[par][:, 0:2048], sig="f0s%d" % par)
            for i in range(16):
                for hh in range(2):
                    par = (i * 2 + hh) % 2
                    p.wait("gpsimd", f0_free[par])
                    t_l = load(lambda par=par: f0b[par][:],
                               lambda i=i, hh=hh: dnw[i * 128:(i + 1) * 128,
                                                      hh * 4096:(hh + 1) * 4096],
                               eng="gpsimd", sig="f0l%d" % par)
                    p.wait("sync", t_l)
                    f0_free[par] = load(lambda i=i, hh=hh: scr_dn[i * 128:(i + 1) * 128,
                                                                 hh * 4096:(hh + 1) * 4096],
                                        lambda par=par: f0b[par][:], sig="f0s%d" % par)
            t_f0_done = [p.barrier_tok("f0s0"), p.barrier_tok("f0s1")]
            t_f0_done = [(s_, v_, "_none_", 16) for (s_, v_, _, _) in t_f0_done]
            expA = s3.enter_context(nc.sbuf_tensor("expA", [128, 2 * 1024], BF16))
            expB = s3.enter_context(nc.sbuf_tensor("expB", [128, 2 * 1024], BF16))
            r_sb = [s3.enter_context(nc.sbuf_tensor(f"r_sb{i}", [1, 1024], F32)) for i in range(2)]
            rbc_sb = [s3.enter_context(nc.sbuf_tensor(f"rbc_sb{i}", [64, 1024], F32)) for i in range(2)]
            pvs_sb = [s3.enter_context(nc.sbuf_tensor(f"pvs_sb{i}", [65, 1024], F32)) for i in range(2)]
            p3 = g.enter_context(ExitStack())
            scA = p3.enter_context(nc.psum_tensor("scA", [128, 1024], F32))
            scB = p3.enter_context(nc.psum_tensor("scB", [128, 1024], F32))
            pvA = p3.enter_context(nc.psum_tensor("pvA", [128, 1024], F32))
            pvB = p3.enter_context(nc.psum_tensor("pvB", [128, 1024], F32))

            t_ready = p.barrier_tok("v")
            sc = {0: scA, 1: scB}
            pv = {0: pvA, 1: pvB}
            ex = {0: expA, 1: expB}
            sc_free = {0: t_ready, 1: t_ready}
            pv_free = {0: None, 1: None}
            fin_free = {0: None, 1: None}
            pend = [None]

            def flush_fin():
                if pend[0] is None:
                    return
                heads_, t_rs = pend[0]
                pend[0] = None
                for hx in (0, 1):
                    h = heads_[hx]
                    p.wait("tensor", t_rs[hx], t_ones)
                    t_rb = None
                    for s_ in range(2):
                        t_rb = p.do("tensor", lambda e, s_=s_, hx=hx: e.matmul(
                            pv[hx][0:64, s_ * 512:(s_ + 1) * 512], ones_f[:, 0:64],
                            r_sb[hx][:, s_ * 512:(s_ + 1) * 512], start=True, stop=True),
                            sig="t" if s_ == 1 else None)
                    p.wait("vector", t_rb)
                    t_rc = p.do("vector", lambda e, hx=hx: e.tensor_copy(
                        rbc_sb[hx][:], pv[hx][0:64, :]), sig="v")
                    p.wait("vector", t_rc)
                    t_fin = p.do("vector", lambda e, hx=hx, h=h: e.tensor_tensor(
                        aoT[64 * (h % 2):64 * (h % 2) + 64, (h // 2) * 1024:(h // 2 + 1) * 1024],
                        pvs_sb[hx][0:64, :], rbc_sb[hx][:], ALU.mult), sig="v")
                    pv_free[hx] = t_rc
                    fin_free[hx] = t_fin
            exp_free = {0: [None, None], 1: [None, None]}
            r_free = None
            t_fin = None

            pairs = []
            for pp in range(4):
                for j in range(4):
                    pairs.append((8 * pp + j, 8 * pp + 4 + j))

            def emit_pv(pi, heads, kc, hx, t_exp_kc):
                g_ = heads[hx] // 4
                p.wait("tensor", t_exp_kc)
                if kc == 0:
                    p.wait("tensor", pv_free[hx])
                t_pv = None
                for s_ in range(2):
                    t_pv = p.do("tensor", lambda e, kc=kc, s_=s_, hx=hx, g_=g_: e.matmul(
                        pv[hx][0:65, s_ * 512:(s_ + 1) * 512],
                        VT[:, kc * 520 + g_ * 65: kc * 520 + g_ * 65 + 65],
                        ex[hx][:, (kc % 2) * 1024 + s_ * 512:(kc % 2) * 1024 + (s_ + 1) * 512],
                        start=(kc == 0), stop=(kc == 31)),
                        sig="t" if s_ == 1 else None)
                exp_free[hx][kc % 2] = t_pv
                return t_pv

            for pi, (hA, hB) in enumerate(pairs):
                heads = {0: hA, 1: hB}
                t_exp = {0: None, 1: None}
                t_exp_prev = {0: None, 1: None}
                t_pv = {0: None, 1: None}
                for kc in range(32):
                    for hx in (0, 1):
                        h = heads[hx]
                        gidx = h // 4
                        ptile = gidx // 2
                        rbase = 64 * (gidx % 2)
                        p.wait("tensor", sc_free[hx])
                        tq = None
                        for s_ in range(2):
                            tq = p.do("tensor", lambda e, kc=kc, s_=s_, hx=hx, ptile=ptile,
                                      rbase=rbase, pi=pi: e.matmul(
                                sc[hx][:, s_ * 512:(s_ + 1) * 512],
                                KT[rbase:rbase + 64,
                                   ptile * 4096 + kc * 128: ptile * 4096 + kc * 128 + 128],
                                QT[rbase:rbase + 64,
                                   pi * 1024 + s_ * 512: pi * 1024 + (s_ + 1) * 512],
                                start=True, stop=True), sig="t" if s_ == 1 else None)
                        p.wait("scalar", tq, t_eb, exp_free[hx][kc % 2])
                        t_exp[hx] = p.do("scalar", lambda e, hx=hx, kc=kc: e.activation(
                            ex[hx][:, (kc % 2) * 1024:(kc % 2 + 1) * 1024], sc[hx][:],
                            AF.Exp, bias=ebias[:], scale=0.125), sig="s")
                        sc_free[hx] = t_exp[hx]
                    if kc == 0:
                        flush_fin()
                    if kc >= 1:
                        for hx in (0, 1):
                            t_pv[hx] = emit_pv(pi, heads, kc - 1, hx, t_exp_prev[hx])
                    t_exp_prev = dict(t_exp)
                for hx in (0, 1):
                    t_pv[hx] = emit_pv(pi, heads, 31, hx, t_exp_prev[hx])
                t_rs = {}
                for hx in (0, 1):
                    p.wait("vector", t_pv[hx], fin_free[hx])
                    t_pvs = p.do("vector", lambda e, hx=hx: e.tensor_copy(
                        pvs_sb[hx][:], pv[hx][0:65, :]), sig="v")
                    pv_free[hx] = t_pvs
                for hx in (0, 1):
                    p.wait("vector", pv_free[hx])
                    t_rs[hx] = p.do("vector", lambda e, hx=hx: e.reciprocal(
                        r_sb[hx][:], pvs_sb[hx][64:65, 0:1024]), sig="v")
                pend[0] = (dict(heads), t_rs)

            flush_fin()
            dbg_dump("aoT", lambda: aoT[:], p.barrier_tok("v"))
            p.flush()
            s3.close()
            p3.close()
            sKV.close()
            full_barrier()
            if BUILD_TO == "a3":
                finish(lambda: xbufs[0][:])
                return nc, dbg_outs

            s4 = g.enter_context(ExitStack())
            wo_sb = s4.enter_context(nc.sbuf_tensor("wo_sb", [128, 16 * 2048], BF16, side="right"))
            xr2t = [s4.enter_context(nc.sbuf_tensor(f"xr2t{i}", [128, 2048], F32, side="right")) for i in range(2)]
            xn2_bf = [s4.enter_context(nc.sbuf_tensor(f"xn2_bf{i}", [128, 2048], BF16, side="right"))
                      for i in range(2)]
            aps = s4.enter_context(nc.sbuf_tensor("aps", [128, 2048], F32, side="right"))
            xn2T = g.enter_context(ExitStack())
            xn2T_sb = xn2T.enter_context(nc.sbuf_tensor("xn2T", [128, 16 * 1024], BF16))
            p4 = g.enter_context(ExitStack())
            woP0 = p4.enter_context(nc.psum_tensor("woP0", [128, 2048], F32))
            woP1 = p4.enter_context(nc.psum_tensor("woP1", [128, 2048], F32))
            woPs = [woP0, woP1]

            t_wo = load(lambda: wo_sb[:].rearrange("p (i c) -> p i c", c=2048),
                        lambda: wo.ap().rearrange("(i p) c -> p i c", p=128),
                        eng="gpsimd", sig="co")
            t_aoT_ready = p.barrier_tok("v")
            if A4_STAGE == 1:
                p.wait("sync", t_wo)
                finish(lambda: xbufs[0][:])
                return nc, dbg_outs

            wo_free = [None, None]
            xb_free4 = [xb_free[0], xb_free[1]]
            stats_free4 = [stats_free[0], stats_free[1]]
            xr2_free = [None, None]
            xn2_free = None
            t_spill = [None] * 8
            t_x2ev = None
            xn2s = [None] * 8

            def a4_mm(tl):
                par = tl % 2
                woP = woPs[par]
                xb = xbufs[par]
                p.wait("sync", xb_free4[par])
                t_x = load(lambda xb=xb: xb[:], lambda tl=tl: x[tl * 128:(tl + 1) * 128, :],
                           sig="lx%d" % par)
                p.wait("tensor", t_wo, t_aoT_ready, wo_free[par])
                t_mm = None
                for ic in range(16):
                    for s_ in range(4):
                        t_mm = p.do("tensor", lambda e, ic=ic, s_=s_, tl=tl, woP=woP: e.matmul(
                            woP[:, s_ * 512:(s_ + 1) * 512],
                            aoT[:, ic * 1024 + tl * 128: ic * 1024 + tl * 128 + 128],
                            wo_sb[:, ic * 2048 + s_ * 512: ic * 2048 + (s_ + 1) * 512],
                            start=(ic == 0), stop=(ic == 15)),
                            sig="t" if (ic == 15 and s_ == 3) else None)
                return t_mm, t_x

            def a4_ln(tl, t_mm, t_x):
                par = tl % 2
                woP = woPs[par]
                xb = xbufs[par]
                xrt = xr2t[par]
                sl = par * 8
                t_var, t_rstd = layernorm_stats(lambda woP=woP: woP[:], sl, (t_mm,),
                                                stats_free4[par], psum_in=True)
                p.wait("vector", t_rstd, t_var)
                t_ap = p.do("vector", lambda e, woP=woP, sl=sl: e.tensor_scalar(
                    aps[:], woP[:], stats[:, sl + 2:sl + 3], stats[:, sl + 6:sl + 7],
                    ALU.subtract, ALU.mult), sig="v")
                p.wait("vector", t_x, t_ap, xr2_free[par],
                       t_spill[tl - 2] if tl >= 2 else None)
                t_xr2 = p.do("vector", lambda e, xrt=xrt, xb=xb: e.tensor_add(
                    xrt[:], aps[:], xb[:]), sig="v")
                xb_free4[par] = t_xr2
                p.wait("sync", t_xr2)
                t_spill[tl] = load(lambda tl=tl: xr2sp[tl * 128:(tl + 1) * 128, :],
                                   lambda xrt=xrt: xrt[:], sig="sp%d" % par)
                t_var2, t_rstd2 = layernorm_stats(lambda xrt=xrt: xrt[:], sl, (t_xr2,), t_ap)
                xnb = xn2_bf[par]
                p.wait("vector", t_rstd2, t_var2, xn2_free[par])
                t_xn2 = p.do("vector", lambda e, xrt=xrt, sl=sl, xnb=xnb: e.tensor_scalar(
                    xnb[:], xrt[:], stats[:, sl + 2:sl + 3], stats[:, sl + 6:sl + 7],
                    ALU.subtract, ALU.mult), sig="v")
                stats_free4[par] = t_xn2
                xr2_free[par] = t_xn2
                xn2s[tl] = (t_xn2, t_ap)

            def a4_tp(tl):
                par = tl % 2
                woP = woPs[par]
                xnb = xn2_bf[par]
                t_xn2, t_ap = xn2s[tl]
                t_tp_last = None
                tp_gate = t_ap
                t_ev = None
                for hb in range(2):
                    p.wait("tensor", t_xn2, tp_gate)
                    t_tp = None
                    for j in range(8):
                        ic = hb * 8 + j
                        t_tp = p.do("tensor", lambda e, ic=ic, j=j, woP=woP, xnb=xnb: e.matmul(
                            woP[:, j * 128:(j + 1) * 128], xnb[:, ic * 128:(ic + 1) * 128],
                            id_bf[:], start=True, stop=True), sig="t" if j == 7 else None)
                    t_tp_last = t_tp
                    p.wait("vector", t_tp)
                    t_ev = p.do("vector", lambda e, hb=hb, tl=tl, woP=woP: e.tensor_copy(
                        bass.AP(xn2T_sb, (hb * 8) * 1024 + tl * 128,
                                [[16 * 1024, 128], [1024, 8], [1, 128]]),
                        woP[:, 0:1024].rearrange("p (i c) -> p i c", c=128)), sig="v")
                    tp_gate = t_ev
                xn2_free[par] = t_tp_last
                wo_free[par] = t_ev
                return t_ev

            xn2_free = [None, None]
            mmres = [None] * 8
            for i in range(9):
                if i < 8:
                    mmres[i] = a4_mm(i)
                if i - 1 >= 0:
                    t_x2ev = a4_tp(i - 1)
                if i < 8:
                    a4_ln(i, *mmres[i])

            dbg_dump("xr2", lambda: xr2t[1][:], p.barrier_tok("v"))
            dbg_dump("xn2T", lambda: xn2T_sb[:], p.barrier_tok("v"))
            p.flush()
            s4.close()
            sAo.close()
            p4.close()
            full_barrier()
            if BUILD_TO == "a4":
                finish(lambda: xbufs[0][:])
                return nc, dbg_outs

            dacc = g.enter_context(ExitStack())
            dacc_sb = dacc.enter_context(nc.sbuf_tensor("dacc", [128, 16 * 1024], F32, side="right"))
            sF = g.enter_context(ExitStack())
            hid = sF.enter_context(nc.sbuf_tensor("hid", [128, 16 * 1024], BF16))
            pan = [sF.enter_context(nc.sbuf_tensor(f"pan{i}", [128, 16 * 512], BF16)) for i in range(2)]
            pF = g.enter_context(ExitStack())
            accP = pF.enter_context(nc.psum_tensor("accP", [128, 2048], F32))
            tpF = pF.enter_context(nc.psum_tensor("tpF", [128, 2048], F32))

            t_xn2T_ready = p.barrier_tok("v")
            p.wait("sync", *t_f0_done)
            pan_free = [None, None]
            pan_sg = [None, None]
            acc_free = [None, None]
            hid_free = None
            t_dacc = [None] * 16
            t_dm = None

            for qt in range(4):
                t_hid = [None] * 16
                for pn in range(4):
                    par = pn % 2
                    p.wait("sync", pan_free[par], pan_sg[par])
                    t_pl = None
                    for ic in range(16):
                        t_pl = p.do("sync", lambda e, ic=ic, pn=pn, qt=qt, par=par: e.dma_start(
                            out=pan[par][:, ic * 512:(ic + 1) * 512],
                            in_=scr_up[qt * 2048 + pn * 512: qt * 2048 + (pn + 1) * 512,
                                       ic * 128:(ic + 1) * 128],
                            transpose=True), sig="tu%d" % par, inc=16)
                    p.wait("scalar", t_pl)
                    t_sg = p.do("scalar", lambda e, par=par: e.activation(
                        pan[par][:], pan[par][:], AF.Sign), sig="s")
                    pan_sg[par] = t_sg
                    t_um = None
                    for hcl in range(4):
                        hc = pn * 4 + hcl
                        hq = qt * 16 + hc
                        apar = hc % 2
                        p.wait("tensor", t_sg, acc_free[apar], t_xn2T_ready)
                        for ic in range(16):
                            for s_ in range(2):
                                t_um = p.do("tensor", lambda e, ic=ic, s_=s_, apar=apar,
                                            par=par, hcl=hcl: e.matmul(
                                    accP[:, apar * 1024 + s_ * 512: apar * 1024 + (s_ + 1) * 512],
                                    pan[par][:, ic * 512 + hcl * 128: ic * 512 + hcl * 128 + 128],
                                    xn2T_sb[:, ic * 1024 + s_ * 512: ic * 1024 + (s_ + 1) * 512],
                                    start=(ic == 0), stop=(ic == 15)),
                                    sig="t" if (ic == 15 and s_ == 1) else None)
                        p.wait("vector", t_um, t_upsl, hid_free)
                        t_hid[hc] = p.do("vector", lambda e, hc=hc, apar=apar, hq=hq: e.tensor_scalar(
                            hid[:, hc * 1024:(hc + 1) * 1024],
                            accP[:, apar * 1024:(apar + 1) * 1024],
                            ups_sb[:, hq:hq + 1], 0.0, ALU.mult, ALU.max), sig="v")
                        acc_free[apar] = t_hid[hc]
                    pan_free[par] = t_um
                t_hid_all = t_hid[15]
                for pn in range(4):
                    par = pn % 2
                    p.wait("sync", pan_free[par], pan_sg[par])
                    t_pl = None
                    for j in range(16):
                        t_pl = p.do("sync", lambda e, j=j, pn=pn, qt=qt, par=par: e.dma_start(
                            out=pan[par][:, j * 512:(j + 1) * 512],
                            in_=scr_dn[pn * 512:(pn + 1) * 512,
                                       (qt * 16 + j) * 128:(qt * 16 + j + 1) * 128],
                            transpose=True), sig="tu%d" % par, inc=16)
                    p.wait("scalar", t_pl)
                    t_sg = p.do("scalar", lambda e, par=par: e.activation(
                        pan[par][:], pan[par][:], AF.Sign), sig="s")
                    pan_sg[par] = t_sg
                    t_dm = None
                    for dcl in range(4):
                        dc = pn * 4 + dcl
                        apar = dc % 2
                        p.wait("tensor", t_sg, t_hid_all, acc_free[apar])
                        for j in range(16):
                            for s_ in range(2):
                                t_dm = p.do("tensor", lambda e, j=j, s_=s_, apar=apar,
                                            par=par, dcl=dcl: e.matmul(
                                    accP[:, apar * 1024 + s_ * 512: apar * 1024 + (s_ + 1) * 512],
                                    pan[par][:, j * 512 + dcl * 128: j * 512 + dcl * 128 + 128],
                                    hid[:, j * 1024 + s_ * 512: j * 1024 + (s_ + 1) * 512],
                                    start=(j == 0), stop=(j == 15)),
                                    sig="t" if (j == 15 and s_ == 1) else None)
                        p.wait("vector", t_dm)
                        if qt == 0:
                            t_dacc[dc] = p.do("vector", lambda e, dc=dc, apar=apar: e.tensor_copy(
                                dacc_sb[:, dc * 1024:(dc + 1) * 1024],
                                accP[:, apar * 1024:(apar + 1) * 1024]), sig="v")
                        else:
                            p.wait("vector", t_dacc[dc])
                            t_dacc[dc] = p.do("vector", lambda e, dc=dc, apar=apar: e.tensor_add(
                                dacc_sb[:, dc * 1024:(dc + 1) * 1024],
                                accP[:, apar * 1024:(apar + 1) * 1024],
                                dacc_sb[:, dc * 1024:(dc + 1) * 1024]), sig="v")
                        acc_free[apar] = t_dacc[dc]
                    pan_free[par] = t_dm
                hid_free = t_dm

            dbg_dump("dacc", lambda: dacc_sb[:], p.barrier_tok("v"))
            p.flush()
            sF.close()
            xn2T.close()
            full_barrier()
            if BUILD_TO == "ffn":
                finish(lambda: xbufs[0][:])
                pF.close()
                dacc.close()
                return nc, dbg_outs

            sfin = dacc.enter_context(nc.sbuf_tensor("sfin", [128, 16], F32, side="right"))
            outb = [dacc.enter_context(nc.sbuf_tensor(f"outb{i}", [128, 2048], F32, side="right")) for i in range(2)]
            aps = dacc.enter_context(nc.sbuf_tensor("apsF", [128, 2048], F32, side="right"))
            p.wait("tensor", t_outsl, t_ones, p.barrier_tok("v"))
            t_os = p.do("tensor", lambda e: e.matmul(
                accP[:, 0:1], ones_f[:, 0:128], outs_sb[:], start=True, stop=True), sig="t")
            p.wait("vector", t_os, t_dnsl)
            t_sf = p.do("vector", lambda e: e.tensor_scalar(
                sfin[:], dns_sb[:], accP[:, 0:1], None, ALU.mult), sig="v")
            t_dsc = [None] * 16
            for dc in range(16):
                p.wait("vector", t_sf, t_dacc[dc])
                t_dsc[dc] = p.do("vector", lambda e, dc=dc: e.tensor_scalar(
                    dacc_sb[:, dc * 1024:(dc + 1) * 1024], dacc_sb[:, dc * 1024:(dc + 1) * 1024],
                    sfin[:, dc:dc + 1], None, ALU.mult), sig="v")

            stats_free6 = [stats_free4[0], stats_free4[1]]
            xb_free6 = [xb_free4[0], xb_free4[1]]
            ftp_free = [None, None]
            aps_free = None
            out_done = [None] * 8
            for tl in range(8):
                par = tl % 2
                sl = par * 8
                xb = xbufs[par]
                p.wait("sync", xb_free6[par], t_spill[tl])
                t_x2 = load(lambda xb=xb: xb[:], lambda tl=tl: xr2sp[tl * 128:(tl + 1) * 128, :],
                            sig="lx%d" % par)
                t_fev = None
                for hb in range(4):
                    p.wait("tensor", ftp_free[par], t_idf, t_os)
                    t_tp = None
                    for j in range(4):
                        dc = hb * 4 + j
                        p.wait("tensor", t_dsc[dc])
                        t_tp = p.do("tensor", lambda e, dc=dc, j=j, par=par, tl=tl: e.transpose(
                            tpF[:, par * 1024 + j * 128: par * 1024 + j * 128 + 128],
                            dacc_sb[:, dc * 1024 + tl * 128: dc * 1024 + tl * 128 + 128],
                            id_f[:]), sig="t" if j == 3 else None)
                    p.wait("vector", t_tp, aps_free)
                    t_fev = p.do("vector", lambda e, hb=hb, par=par: e.tensor_copy(
                        aps[:, hb * 512:(hb + 1) * 512],
                        tpF[:, par * 1024: par * 1024 + 512]), sig="v")
                    ftp_free[par] = t_fev
                t_var, t_rstd = layernorm_stats(lambda: aps[:], sl, (t_fev,), stats_free6[par])
                p.wait("vector", t_rstd, t_var, t_x2, out_done[tl - 2] if tl >= 2 else None)
                t_o = p.do("vector", lambda e, par=par, sl=sl: e.tensor_scalar(
                    outb[par][:], aps[:], stats[:, sl + 2:sl + 3], stats[:, sl + 6:sl + 7],
                    ALU.subtract, ALU.mult), sig="v")
                aps_free = t_o
                p.wait("vector", t_o)
                t_o2 = p.do("vector", lambda e, par=par, xb=xb: e.tensor_add(
                    outb[par][:], outb[par][:], xb[:]), sig="v")
                stats_free6[par] = t_o2
                xb_free6[par] = t_o2
                p.wait("sync", t_o2)
                out_done[tl] = p.do("sync", lambda e, tl=tl, par=par: e.dma_start(
                    out=out[tl * 128:(tl + 1) * 128, :], in_=outb[par][:]), sig="so%d" % par, inc=16)

            all_dma_barrier("sync")
            p.flush()
            pF.close()
            dacc.close()
    return nc, dbg_outs


def _tables(qc):
    pos = (np.arange(4096) + qc * 1024) % 4096
    f = np.arange(32)
    inv = 10000.0 ** (-2.0 * f / 64.0)
    ang = pos[:, None] * inv[None, :]
    cos = np.cos(ang).astype(np.float32)
    sin = np.sin(ang).astype(np.float32)
    cosk = np.zeros((128, 1024), np.float32)
    sink = np.zeros((128, 1024), np.float32)
    for t in range(32):
        cosk[:, t * 32:(t + 1) * 32] = cos[t * 128:(t + 1) * 128]
        sink[:, t * 32:(t + 1) * 32] = sin[t * 128:(t + 1) * 128]
    return cosk, sink


def make_in_maps(L0_attn_x, L0_attn_wq, L0_attn_wk, L0_attn_wv, L0_attn_wo,
                 L0_ffn_up_weight, L0_ffn_down_weight, L0_ffn_up_scale,
                 L0_ffn_down_scale, L0_ffn_output_scale):
    in_maps = []
    for c in range(8):
        b, qc = c // 4, c % 4
        xr = np.roll(np.asarray(L0_attn_x[b]), -qc * 1024, axis=0)
        ck, sk = _tables(qc)
        in_maps.append({
            "x": np.ascontiguousarray(xr),
            "wq": np.asarray(L0_attn_wq), "wk": np.asarray(L0_attn_wk),
            "wv": np.asarray(L0_attn_wv), "wo": np.asarray(L0_attn_wo),
            "upw": np.asarray(L0_ffn_up_weight), "dnw": np.asarray(L0_ffn_down_weight),
            "ups": np.ascontiguousarray(np.asarray(L0_ffn_up_scale).reshape(64, 128).T),
            "dns": np.ascontiguousarray(np.asarray(L0_ffn_down_scale).reshape(16, 128).T),
            "outsc": np.asarray(L0_ffn_output_scale).reshape(1, 1),
            "cosk": ck, "sink": sk,
            "ident": np.eye(128, dtype=np.float32),
            "onesr": np.ones((1, 128), np.float32),
        })
    return in_maps


def kernel(**inputs):
    nc, _ = build_nc()
    in_maps = make_in_maps(**inputs)
    res = run_bass_kernel_spmd(nc, in_maps, core_ids=list(range(8)))
    outp = np.empty((2, 4096, 2048), np.float32)
    for c in range(8):
        b, qc = c // 4, c % 4
        outp[b, qc * 1024:(qc + 1) * 1024] = res.results[c]["out"]
    return outp
